# version 24
# baseline (speedup 1.0000x reference)
import numpy as np
from contextlib import ExitStack
import concourse.bass as bass
import concourse.mybir as mybir
from concourse.bass_utils import run_bass_kernel_spmd

F32 = mybir.dt.float32
BF16 = mybir.dt.bfloat16
U8 = mybir.dt.uint8
I8 = mybir.dt.int8
I32 = mybir.dt.int32
AF = mybir.ActivationFunctionType
ALU = mybir.AluOpType
AX = mybir.AxisListType.X

B, T, D, H, HD, FF = 16, 2048, 1024, 16, 64, 4096
BPC = 2
TPC = BPC * T
SCT = 512
NSC_B = T // SCT
CC = 128
NCC = SCT // CC
NE = D // 128
NF = FF // 128
C1_1024 = float(np.float32(1.0 / 1024.0))
C255I = float(np.float32(1.0 / 255.0))
C127I = float(np.float32(1.0 / 127.0))

ENGINES = ("sync", "vector", "scalar", "tensor", "gpsimd")


class Prog:
    def __init__(self, nc, ctx):
        self.nc = nc
        self.steps = {e: [] for e in ENGINES}
        self.semh = {}
        self.semc = {}
        for name in ("sv", "sa", "st", "sg"):
            self.semh[name] = ctx.enter_context(nc.semaphore(name))
            self.semc[name] = 0
        self.dma_sems = set()
        self._ctx = ctx
        self.last_tok = {"vector": None, "scalar": None, "gpsimd": None}
        self.eng_sem = {"vector": "sv", "scalar": "sa", "tensor": "st",
                        "gpsimd": "sg"}
        self.waited = {e: {} for e in ENGINES}

    def _op(self, eng, fn, wait, inc, inc_n=1, hard=False):
        wait = list(wait)
        if eng in self.last_tok and fn is not None:
            wait.append(self.last_tok[eng])
        waits = []
        for tok in wait:
            if tok is None:
                continue
            (sname, v, thard) = tok
            if sname in self.dma_sems:
                v = self.semc[sname]
            if v <= 0:
                continue
            if self.waited[eng].get(sname, 0) >= v:
                continue
            self.waited[eng][sname] = v
            waits.append((self.semh[sname], v))
        tok = None
        incs = []
        if inc is not None:
            self.semc[inc] += inc_n
            tok = (inc, self.semc[inc], hard)
            incs = [(self.semh[inc], inc_n)]
        self.steps[eng].append((waits, fn, incs))
        if eng in self.last_tok and fn is not None and tok is not None:
            self.last_tok[eng] = tok
        return tok

    def t(self, fn, wait=(), inc=False):
        return self._op("tensor", fn, wait, "st" if inc else None)

    def v(self, fn, wait=(), inc=True, hard=False):
        return self._op("vector", fn, wait, "sv" if inc else None, hard=hard)

    def a(self, fn, wait=(), inc=True, hard=False):
        return self._op("scalar", fn, wait, "sa" if inc else None, hard=hard)

    def g(self, fn, wait=(), inc=True, hard=False):
        return self._op("gpsimd", fn, wait, "sg" if inc else None, hard=hard)

    def dma(self, eng, out, in_, wait=(), transpose=False, sem="dmisc"):
        if sem not in self.semh:
            self.semh[sem] = self._ctx.enter_context(self.nc.semaphore(sem))
            self.semc[sem] = 0
            self.dma_sems.add(sem)
        if transpose:
            fn = (lambda e, o=out, i=in_: e.dma_start(out=o, in_=i, transpose=True))
        else:
            fn = (lambda e, o=out, i=in_: e.dma_start(out=o, in_=i))
        return self._op(eng, fn, wait, sem, inc_n=16, hard=True)

    def wait(self, eng, toks):
        self._op(eng, None, toks, None)

    def emit(self, block):
        for ename in ENGINES:
            steps = self.steps[ename]

            def emitter(e, steps=steps):
                for waits, fn, incs in steps:
                    for (hh, vv) in waits:
                        e.wait_ge(hh, vv)
                    ins = fn(e) if fn is not None else None
                    for (hh, nn) in incs:
                        ins.then_inc(hh, nn)

            getattr(block, ename)(emitter)


class MK:
    def __init__(self, upto="p4", tap=None, nb=BPC, nsc=NSC_B, part="wo",
                 ncc=NCC):
        self.upto = upto
        self.tap = tap
        self.nb = nb
        self.nsc = nsc
        self.part = part
        self.ncc = ncc
        self.nc = bass.Bass("TRN2", target_bir_lowering=False, debug=False)
        self.taps = {}

    def mktap(self, name, shape, dt=F32):
        ap = self.nc.dram_tensor("tap_" + name, shape, dt, kind="ExternalOutput").ap()
        self.taps[name] = ap
        return ap

    def build(self):
        nc = self.nc
        self.x_h = nc.dram_tensor("x", [BPC, T, D], F32, kind="ExternalInput").ap()
        self.w_h = [nc.dram_tensor(n, [D, D], F32, kind="ExternalInput").ap()
                    for n in ("wq", "wk", "wv", "wo")]
        self.w1_h = nc.dram_tensor("w1", [FF, D], F32, kind="ExternalInput").ap()
        self.b1_h = nc.dram_tensor("b1", [FF], F32, kind="ExternalInput").ap()
        self.w2_h = nc.dram_tensor("w2", [D, FF], F32, kind="ExternalInput").ap()
        self.b2_h = nc.dram_tensor("b2", [D], F32, kind="ExternalInput").ap()
        self.c_h = nc.dram_tensor("c", [BPC, D], F32, kind="ExternalInput").ap()
        self.wmod_h = nc.dram_tensor("wmod", [D, 6 * D], F32,
                                     kind="ExternalInput").ap()
        self.out_h = nc.dram_tensor("out", [BPC, T, D], F32,
                                    kind="ExternalOutput").ap()
        self.xb_h = nc.dram_tensor("xbounce", [TPC, D], BF16).ap()
        self.w1b_h = nc.dram_tensor("w1bounce", [FF, D], BF16).ap()
        self.w2b_h = nc.dram_tensor("w2bounce", [D, FF], BF16).ap()
        self.mid_h = nc.dram_tensor("mid", [FF, TPC], F32).ap()
        self.rows_h = nc.dram_tensor("rows", [8, T], BF16).ap()
        self.cci_h = [nc.dram_tensor(f"cc{i}in", [1, 4], F32).ap() for i in (1, 2)]
        self.cco_h = [nc.dram_tensor(f"cc{i}out", [1, 4], F32).ap() for i in (1, 2)]

        with ExitStack() as ctx:
            self.ctx = ctx
            p = Prog(nc, ctx)
            self.p = p
            sb = lambda n, s, dt: ctx.enter_context(nc.sbuf_tensor(n, s, dt))
            ps = lambda n, s, dt: ctx.enter_context(nc.psum_tensor(n, s, dt))
            self.sb = sb

            self.XT = sb("XT", [128, NE * TPC], BF16)
            self.WA = sb("WA", [128, 4 * NE * D], BF16)
            self.ones_col = sb("ones_col", [128, 1], BF16)
            self.ones_rowf = sb("ones_rowf", [1, 128], F32)
            self.identb = sb("identb", [128, 128], BF16)
            self.identf = sb("identf", [128, 128], F32)
            self.maskb = sb("maskb", [128, 128], BF16)
            self.modcol = sb("modcol", [128, 96], F32)
            self.onepca = sb("onepca", [128, 8 * BPC], F32)
            self.sacol = sb("sacol", [128, 8 * BPC], F32)
            self.onepcb = sb("onepcb", [128, 8 * BPC], F32)
            self.sbcol = sb("sbcol", [128, 8 * BPC], F32)
            self.gacol = sb("gacol", [128, 8 * BPC], F32)
            self.gbcol = sb("gbcol", [128, 8 * BPC], F32)
            self.gtmp = sb("gtmp", [128, 8 * BPC], F32)
            self.gtmp2 = sb("gtmp2", [128, 8 * BPC], F32)
            self.gmx = sb("gmx", [128, 1], F32)
            self.ROWM = sb("ROWM", [1, SCT], F32)
            self.ROWQ = sb("ROWQ", [1, SCT], F32)
            self.ROWT1 = sb("ROWT1", [1, SCT], F32)
            self.ROWT2 = sb("ROWT2", [1, SCT], F32)
            self.ROWB1 = sb("ROWB1", [1, SCT], BF16)
            self.ROWB2 = sb("ROWB2", [1, SCT], BF16)
            self.sw1b = sb("sw1b", [128, 32], F32)
            self.sw2b = sb("sw2b", [128, 8], F32)
            self.b1c = sb("b1c", [128, 32], F32)
            self.b2c = sb("b2c", [128, 8], F32)
            self.bq1c = sb("bq1c", [128, 32], F32)
            self.sc1c = sb("sc1c", [128, 32], F32)
            self.gsc2 = sb("gsc2", [128, 8 * BPC], F32)
            self.gbq2 = sb("gbq2", [128, 8 * BPC], F32)
            self.qt1 = sb("qt1", [128, 32], F32)
            self.qt2 = sb("qt2", [128, 32], F32)
            self.qi32 = sb("qi32", [128, 32], I32)
            self.scl = sb("scl", [128, 4], F32)
            self.mx1 = sb("mx1", [128, 1], F32)
            self.mx2 = sb("mx2", [128, 1], F32)
            self.maxP = sb("maxP", [128, 16], BF16)
            self.minP = sb("minP", [128, 16], BF16)
            self.rc1b = sb("rc1b", [128, 1], BF16)
            self.mpR = sb("mpR", [128, SCT], BF16)
            self.mpB = sb("mpB", [128, SCT], BF16)
            self.mptm = sb("mptm", [128, SCT], BF16)
            self.mstg = [sb("mstg%d" % i, [128, SCT], F32) for i in range(2)]
            self.qacc = sb("qacc", [128, 1], F32)
            self.wqscl = sb("wqscl", [128, 1], F32)
            self.ascol = sb("ascol", [128, 8 * BPC], F32)
            self.bscol = sb("bscol", [128, 8 * BPC], F32)
            self.t8a = sb("t8a", [128, 8], F32)
            self.t8b = sb("t8b", [128, 8], F32)
            self.rc1 = sb("rc1", [128, 1], F32)
            self.ccs = sb("ccs", [1, 4], F32)
            self.silT = sb("silT", [128, NE * BPC], F32)
            xsq1 = sb("xsq0", [128, SCT], BF16)
            self._xsq = [xsq1, xsq1]
            self._xsq_free = [None, None]

            self.A01 = ps("A01", [128, 1024], F32)
            self.YB = ps("YB", [128, 1024], F32)
            self.TB = ps("TB", [128, 1024], BF16)
            self.DB = ps("DB", [128, 512], F32)
            self.MS = ps("MS", [128, 512], F32)
            self.P8 = ps("P8", [128, 512], F32)

            self.p0()
            if self.upto != "p0":
                self.p1_all()
            if self.upto in ("p2", "p3", "p4"):
                self.p2()
            if self.upto in ("p3", "p4"):
                self.p3()
            if self.upto == "p4":
                self.p4()
            self.finale()

            with nc.Block() as block:
                p.emit(block)
        return nc

    def p0(self):
        p = self.p
        p0s = ExitStack()
        s0 = lambda n, s_, dt: p0s.enter_context(self.nc.sbuf_tensor(n, s_, dt))
        self.c_sb = s0("c_sb", [BPC, D], F32)
        self.sil = s0("sil", [BPC, D], F32)
        self.modrow = s0("modrow", [BPC, 6 * D], F32)
        b1s = s0("b1s", [32, 128], F32)
        b2s = s0("b2s", [8, 128], F32)
        wmst = [s0("wmst%d" % i, [128, NE * 256], F32) for i in range(2)]

        self.tok_ones = p.g(lambda e: e.memset(self.ones_col[:, :], 1.0),
                            inc=True, hard=True)
        self.tok_onesrf = p.g(lambda e: e.memset(self.ones_rowf[:, :], 1.0),
                              inc=True, hard=True)
        t1 = p.g(lambda e: e.memset(self.identb[:, :], 1.0), inc=True, hard=True)
        self.tok_ib = p.g(lambda e: e.affine_select(
            self.identb[:, :], self.identb[:, :], pattern=[[1, 128]],
            compare_op=ALU.is_equal, fill=0.0, channel_multiplier=-1),
            wait=[t1], inc=True, hard=True)
        t2 = p.g(lambda e: e.memset(self.identf[:, :], 1.0), inc=True, hard=True)
        self.tok_if = p.g(lambda e: e.affine_select(
            self.identf[:, :], self.identf[:, :], pattern=[[1, 128]],
            compare_op=ALU.is_equal, fill=0.0, channel_multiplier=-1),
            wait=[t2], inc=True, hard=True)
        t3 = p.g(lambda e: e.memset(self.maskb[:, :], 1.0), inc=True, hard=True)
        self.tok_mask = p.g(lambda e: e.affine_select(
            self.maskb[:, :], self.maskb[:, :], pattern=[[1, 128]],
            compare_op=ALU.is_ge, fill=0.0, channel_multiplier=-1),
            wait=[t3], inc=True, hard=True)
        self.tok_mx0 = p.g(lambda e: e.memset(self.mx1[:, :], -1e30),
                           inc=True, hard=True)
        self.tok_mx20 = p.g(lambda e: e.memset(self.mx2[:, :], -1e30),
                            inc=True, hard=True)
        self.tok_ccs0 = p.g(lambda e: e.memset(self.ccs[:, :], 0.0),
                            inc=True, hard=True)

        xflat = self.x_h.rearrange("b t d -> (b t) d")
        self.tok_xb = []

        def xb_ring(rc):
            self.tok_xb.append(p.dma(
                "gpsimd", self.xb_h[rc * 1024:(rc + 1) * 1024, :],
                xflat[rc * 1024:(rc + 1) * 1024, :], sem="dxb%d" % rc))

        self.wtoks = []

        def wa_ring(i):
            dst = self.WA[:, i * 8192:(i + 1) * 8192].rearrange(
                "p (je d) -> p je d", je=NE)
            src = self.w_h[i].rearrange("(je p) d -> p je d", p=128)
            self.wtoks.append(p.dma("gpsimd", dst, src, sem="dwa"))

        tok_c = p.dma("sync", self.c_sb[:, :], self.c_h, sem="dcb")
        tok_b1 = p.dma("sync", b1s[:, :], self.b1_h.rearrange("(a j) -> a j", j=128),
                       sem="dcb")
        tok_b2 = p.dma("sync", b2s[:, :], self.b2_h.rearrange("(a j) -> a j", j=128),
                       sem="dcb")
        wmsrc = self.wmod_h.rearrange("(je p) f -> p je f", p=128)

        def wm_dma(r, w):
            buf = wmst[r % 2]
            return p.dma("sync", buf.ap().rearrange("p (je f) -> p je f", je=NE),
                         wmsrc[:, :, r * 256:(r + 1) * 256], wait=w,
                         sem="dwm%d" % (r % 2))
        wm_toks = [wm_dma(0, ()), wm_dma(1, ())]
        xb_ring(0)
        wa_ring(0)

        self.xt_toks = []

        def xt_ring(rc):
            tk = None
            for jf in range(NE):
                tk = p.dma(
                    "sync",
                    self.XT[:, jf * TPC + rc * 1024:jf * TPC + (rc + 1) * 1024],
                    self.xb_h[rc * 1024:(rc + 1) * 1024,
                              jf * 128:(jf + 1) * 128],
                    wait=[self.tok_xb[rc]] if jf == 0 else (),
                    transpose=True, sem="dxt%d" % rc)
            self.xt_toks.append(tk)
        xt_ring(0)

        tb1 = p.t(lambda e: e.transpose(self.MS[:, 0:32], b1s[:, :],
                                        self.identf[0:32, 0:32]),
                  wait=[tok_b1, self.tok_if], inc=True)
        self.tok_b1c = p.v(lambda e: e.tensor_copy(self.b1c[:, :], self.MS[:, 0:32]),
                           wait=[tb1], inc=True, hard=True)
        tb2 = p.t(lambda e: e.transpose(self.MS[:, 32:40], b2s[:, :],
                                        self.identf[0:8, 0:8]),
                  wait=[tok_b2, self.tok_b1c], inc=True)
        self.tok_b2c = p.v(lambda e: e.tensor_copy(self.b2c[:, :], self.MS[:, 32:40]),
                           wait=[tb2], inc=True, hard=True)

        tok_sil = p.a(lambda e: e.activation(self.sil[:, :], self.c_sb[:, :],
                                             AF.Silu), wait=[tok_c], inc=True)
        ttr = None
        for je in range(NE):
            ttr = p.t(lambda e, je=je: e.transpose(
                self.MS[:, 40 + je * BPC:40 + (je + 1) * BPC],
                self.sil[:, je * 128:(je + 1) * 128], self.identf[0:BPC, 0:BPC]),
                wait=[tok_sil, self.tok_b2c], inc=(je == NE - 1))
        tok_silT = p.v(lambda e: e.tensor_copy(self.silT[:, :],
                                               self.MS[:, 40:40 + NE * BPC]),
                       wait=[ttr], inc=True, hard=True)

        self.bank = {"A01": None, "YB": None, "DS": None, "TB": None,
                     "DB": None, "MS": None, "P8": None}
        self.p0_done = []
        self.tok_rows1 = {0: [], 1: []}
        self._rstd_war = []
        self._nmr_war = []
        self.fillers_hi = []
        self.fillers_lo = []
        self._xt_ring = xt_ring

        def sp1(b, pc, deps):
            rt = self.ln_stats_piece(b * T, pc, deps, 2 * b, "drow%d" % b,
                                     self.tok_rows1[b],
                                     dmaeng="sync" if b == 0 else "gpsimd")
            if b == 0:
                self.p0_done += rt

        def mslice(vec, b):
            a = self.modcol.ap().rearrange("p (v j b2) -> p v j b2", v=6, j=8)
            return a[:, vec, :, b]

        self._gate_last = None
        self._ms_ev = None

        def vec_T(v, etok):
            mt = None
            for je in range(NE):
                k = v * 8 + je
                mt = p.t(lambda e, v=v, je=je, k=k: e.transpose(
                    self.MS[:, 100 + k * BPC:100 + (k + 1) * BPC],
                    self.modrow[:, v * D + je * 128: v * D + (je + 1) * 128],
                    self.identf[0:BPC, 0:BPC]),
                    wait=[etok, tok_silT, self.tok_if] if je == 0 else (),
                    inc=(je == NE - 1))
            ev = p.v(lambda e, v=v: e.tensor_copy(
                self.modcol[:, v * 16:(v + 1) * 16],
                self.MS[:, 100 + v * 16:100 + (v + 1) * 16]),
                wait=[mt], inc=True, hard=True)
            self._ms_ev = ev
            return ev

        def gate_chain(v, dcol, ev):
            last = self._gate_last
            for b in range(BPC):
                dst = slice(b * 8, b * 8 + 8)
                c0 = p.v(lambda e, v=v, b=b, dst=dst: e.tensor_copy(
                    self.gtmp2[:, dst], mslice(v, b)),
                    wait=[ev, last], hard=True)
                c1 = p.v(lambda e, dst=dst: e.tensor_scalar(
                    self.gtmp[:, dst], self.gtmp2[:, dst], -1.0, None, ALU.mult),
                    wait=[c0], hard=True)
                c2 = p.v(lambda e, dst=dst: e.tensor_tensor(
                    self.gtmp[:, dst], self.gtmp[:, dst], self.gtmp2[:, dst],
                    ALU.max), wait=[c1], hard=True)
                c3 = p.v(lambda e, dst=dst: e.tensor_scalar(
                    self.gtmp[:, dst], self.gtmp[:, dst], 0.25, None, ALU.max),
                    wait=[c2], hard=True)
                c4 = p.v(lambda e, dst=dst: e.tensor_scalar(
                    self.gtmp2[:, dst], self.gtmp2[:, dst], 0.0, None,
                    ALU.is_ge), wait=[c3], hard=True)
                c5 = p.v(lambda e, dst=dst: e.tensor_scalar(
                    self.gtmp2[:, dst], self.gtmp2[:, dst], 2.0, -1.0,
                    ALU.mult, ALU.add), wait=[c4], hard=True)
                last = p.v(lambda e, dcol=dcol, dst=dst: e.tensor_mul(
                    dcol[:, dst], self.gtmp[:, dst], self.gtmp2[:, dst]),
                    wait=[c5], inc=True, hard=True)
            self._gate_last = last
            return last

        def vec_gates(v, ev):
            if v in (0, 3):
                dcol = self.sacol if v == 0 else self.sbcol
                tk = None
                for b in range(BPC):
                    dst = slice(b * 8, b * 8 + 8)
                    tk = p.v(lambda e, v=v, b=b, dst=dst, dcol=dcol:
                             e.tensor_copy(dcol[:, dst], mslice(v, b)),
                             wait=[ev], inc=(b == BPC - 1), hard=True)
                if v == 0:
                    self._tok_sa = tk
            elif v in (1, 4):
                dcol = self.onepca if v == 1 else self.onepcb
                tk = None
                for b in range(BPC):
                    dst = slice(b * 8, b * 8 + 8)
                    tk = p.v(lambda e, v=v, b=b, dst=dst, dcol=dcol:
                             e.tensor_scalar(dcol[:, dst], mslice(v, b),
                                             1.0, None, ALU.add),
                             wait=[ev], inc=(b == BPC - 1), hard=True)
                if v == 1:
                    self.tok_h1mod = tk
            elif v == 2:
                self.tok_ga = gate_chain(2, self.gacol, ev)
            else:
                self.tok_gb = gate_chain(5, self.gbcol, ev)

        extras = {0: lambda: xb_ring(1), 2: lambda: wa_ring(1),
                  6: lambda: wa_ring(2), 8: lambda: xb_ring(2),
                  10: lambda: wa_ring(3), 12: lambda: xb_ring(3)}
        etok = None
        for r in range(24):
            buf = wmst[r % 2]
            mmtok = None
            for je in range(NE):
                mmtok = p.t(lambda e, je=je, buf=buf: e.matmul(
                    self.DB[0:BPC, 0:256], self.silT[:, je * BPC:(je + 1) * BPC],
                    buf[:, je * 256:(je + 1) * 256],
                    start=(je == 0), stop=(je == NE - 1)),
                    wait=([wm_toks[r], tok_silT, etok] if je == 0 else ()),
                    inc=(je == NE - 1))
            etok = p.v(lambda e, r=r: e.tensor_copy(
                self.modrow[:, r * 256:(r + 1) * 256], self.DB[0:BPC, 0:256]),
                wait=[mmtok], inc=True, hard=True)
            if r + 2 < 24:
                wm_toks.append(wm_dma(r + 2, [mmtok]))
            if r in extras:
                extras[r]()
            if r == 2:
                sp1(0, 0, [])
            elif r == 6:
                sp1(0, 1, [])
            if (r + 1) % 4 == 0:
                vec_done = vec_T((r + 1) // 4 - 1, etok)
                vec_gates((r + 1) // 4 - 1, vec_done)
        tok_mod = etok
        xt_ring(1)
        sp1(0, 2, [])
        sp1(0, 3, [])
        self.tok_gates = self.tok_gb

        self.fillers_hi.append((0.0, lambda: xt_ring(2)))
        self.fillers_hi.append((0.0, lambda: xt_ring(3)))
        for pc in range(NSC_B):
            self.fillers_hi.append((6.0, lambda pc=pc: sp1(1, pc, [])))

        self.bank["DB"] = tok_mod
        self.bank["MS"] = self._ms_ev
        self.p0_done += [tok_mod, self._ms_ev, self.tok_gb, tok_silT,
                         self.tok_b1c, self.tok_b2c]

        if self.tap == "p0":
            p.dma("sync", self.mktap("mod", [BPC, 6 * D]), self.modrow[:, :],
                  wait=[tok_mod])
            p.dma("sync", self.mktap("gates", [128, 8 * BPC]), self.gacol[:, :],
                  wait=[self.tok_gates])
            p.dma("sync", self.mktap("gatesb", [128, 8 * BPC]), self.gbcol[:, :],
                  wait=[self.tok_gates])
            p.dma("sync", self.mktap("xt", [128, 1024], BF16), self.XT[:, 0:1024],
                  wait=[self.xt_toks[0]])
            p.dma("sync", self.mktap("rows", [8, T], BF16), self.rows_h,
                  wait=self.tok_rows1[0] + self.tok_rows1[1])
            p.dma("sync", self.mktap("b1c", [128, 32]), self.b1c[:, :],
                  wait=[self.tok_b1c])
            p.dma("sync", self.mktap("mrow", [1, SCT]), self.ROWM[:, :],
                  wait=list(self.tok_rows1[0]))
            p.dma("sync", self.mktap("qrow", [1, SCT]), self.ROWQ[:, :],
                  wait=list(self.tok_rows1[0]))

        p0s.close()

    def ln_stats(self, xoff, first_deps):
        out_toks = []
        deps = list(first_deps)
        for piece in range(NSC_B):
            out_toks += self.ln_stats_piece(xoff, piece, deps)
            deps = []
        return out_toks

    def ln_stats_piece(self, xoff, piece, deps, rowbase, sem, outlist,
                       dmaeng="sync"):
        p = self.p
        xsq = self._xsq
        bank_free = self.bank.get("P8")
        g0 = xoff + piece * SCT
        xtk = self.xt_toks[g0 // 1024]
        mm_s = None
        for je in range(NE):
            xs = self.XT[:, je * TPC + g0: je * TPC + g0 + SCT]
            mm_s = p.t(lambda e, xs=xs, je=je: e.matmul(
                self.P8[0:1, 0:SCT], self.ones_col[:, :], xs,
                start=(je == 0), stop=(je == NE - 1)),
                wait=[self.tok_ones, xtk] +
                     (deps + [bank_free] if je == 0 else []),
                inc=(je == NE - 1))
        t1 = p.v(lambda e: e.tensor_scalar(self.ROWM[:, :], self.P8[0:1, 0:SCT],
                                           C1_1024, None, ALU.mult),
                 wait=[mm_s], inc=True, hard=True)
        mm_q = None
        for je in range(NE):
            xs = self.XT[:, je * TPC + g0: je * TPC + g0 + SCT]
            sq = xsq[je % 2]
            w_sq = [xtk, self._xsq_free[0]] + (deps if je == 0 else [])
            sqt = p.a(lambda e, xs=xs, sq=sq: e.activation(sq[:, :], xs,
                                                           AF.Square),
                      wait=w_sq, inc=True, hard=True)
            mm_q = p.t(lambda e, sq=sq, je=je: e.matmul(
                self.P8[0:1, 0:SCT], self.ones_col[:, :], sq[:, :],
                start=(je == 0), stop=(je == NE - 1)),
                wait=[sqt] + ([t1] if je == 0 else []),
                inc=True)
            self._xsq_free[0] = mm_q
        t2 = p.v(lambda e: e.tensor_scalar(self.ROWQ[:, :], self.P8[0:1, 0:SCT],
                                           C1_1024, None, ALU.mult),
                 wait=[mm_q], inc=True, hard=True)
        self.bank["P8"] = t2
        t3 = p.v(lambda e: e.tensor_mul(self.ROWT1[:, :], self.ROWM[:, :],
                                        self.ROWM[:, :]),
                 wait=[t1], hard=True)
        t3b = p.v(lambda e: e.tensor_sub(self.ROWT2[:, :], self.ROWQ[:, :],
                                         self.ROWT1[:, :]),
                  wait=[t2, t3], hard=True)
        t4 = p.v(lambda e: e.tensor_scalar(self.ROWT2[:, :], self.ROWT2[:, :],
                                           1e-5, None, ALU.add),
                 wait=[t3b], inc=True, hard=True)
        t5 = p.a(lambda e: e.activation(self.ROWT1[:, :], self.ROWT2[:, :],
                                        AF.Sqrt), wait=[t4], inc=True,
                 hard=True)
        t6 = p.v(lambda e: e.reciprocal(self.ROWT2[:, :], self.ROWT1[:, :]),
                 wait=[t5], inc=True, hard=True)
        t7 = p.v(lambda e: e.tensor_copy(self.ROWB2[:, :], self.ROWT2[:, :]),
                 wait=[t6] + self._rstd_war, inc=True, hard=True)
        d1 = p.dma(dmaeng, self.rows_h[rowbase:rowbase + 1,
                                       piece * SCT:(piece + 1) * SCT],
                   self.ROWB2[:, :], wait=[t7], sem=sem)
        t8 = p.v(lambda e: e.tensor_mul(self.ROWT1[:, :], self.ROWM[:, :],
                                        self.ROWT2[:, :]), wait=[t6],
                 hard=True)
        t9 = p.v(lambda e: e.tensor_scalar(self.ROWT1[:, :], self.ROWT1[:, :],
                                           -1.0, None, ALU.mult),
                 wait=[t8], inc=True, hard=True)
        t9b = p.v(lambda e: e.tensor_copy(self.ROWB1[:, :], self.ROWT1[:, :]),
                  wait=[t9] + self._nmr_war, inc=True, hard=True)
        d2 = p.dma(dmaeng, self.rows_h[rowbase + 1:rowbase + 2,
                                       piece * SCT:(piece + 1) * SCT],
                   self.ROWB1[:, :], wait=[t9b], sem=sem)
        self._rstd_war = [d1]
        self._nmr_war = [d2]
        outlist += [d1, d2]
        return [d1, d2]

    def _pop_fillers(self, budget=45.0):
        while (self.fillers_hi or self.fillers_lo) and budget > 0:
            q = self.fillers_hi if self.fillers_hi else self.fillers_lo
            cost, fn = q.pop(0)
            fn()
            budget -= max(cost, 0.5)

    def _flush_fillers(self):
        while self.fillers_hi or self.fillers_lo:
            q = self.fillers_hi if self.fillers_hi else self.fillers_lo
            q.pop(0)[1]()

    def _sp2_job(self, b, pc):
        deps = list(self.sc_res[(b, pc)])
        self.ln_stats_piece(b * T, pc, deps, 4 + 2 * b, "drw2%d" % b,
                            self.tok_rows2[b], dmaeng="gpsimd")

    def _mp_load(self, b, sc):
        p = self.p
        rbw = self.tok_rows2[b][2 * sc:2 * sc + 2] + self._mp_war
        rb1 = p.dma("gpsimd", self.mpR[:, :],
                    self.rows_h[4 + 2 * b:5 + 2 * b,
                                sc * SCT:(sc + 1) * SCT].broadcast_to(
                                    [128, SCT]), wait=rbw, sem="drb2")
        rb2 = p.dma("gpsimd", self.mpB[:, :],
                    self.rows_h[5 + 2 * b:6 + 2 * b,
                                sc * SCT:(sc + 1) * SCT].broadcast_to(
                                    [128, SCT]), sem="drb2")
        self._mp_rb = [rb1, rb2]

    def _mp_je(self, b, sc, je):
        p = self.p
        g0 = b * T + sc * SCT
        cl = b * 8 + je
        xs = self.XT[:, je * TPC + g0: je * TPC + g0 + SCT]
        res = list(self.sc_res[(b, sc)]) if je == 0 else []
        u1 = p.v(lambda e, xs=xs: e.tensor_mul(self.mptm[:, :], xs,
                                               self.mpR[:, :]),
                 wait=self._mp_rb + res, hard=True)
        u2 = p.v(lambda e: e.tensor_add(self.mptm[:, :], self.mptm[:, :],
                                        self.mpB[:, :]), wait=[u1],
                 hard=True)
        u3 = p.v(lambda e: e.tensor_reduce(self.rc1b[:, :], self.mptm[:, :],
                                           AX, ALU.max), wait=[u2],
                 hard=True)
        if sc == 0:
            u4 = p.v(lambda e, cl=cl: e.tensor_copy(
                self.maxP[:, cl:cl + 1], self.rc1b[:, :]), wait=[u3],
                hard=True)
        else:
            u4 = p.v(lambda e, cl=cl: e.tensor_tensor(
                self.maxP[:, cl:cl + 1], self.maxP[:, cl:cl + 1],
                self.rc1b[:, :], ALU.max), wait=[u3], hard=True)
        u5 = p.v(lambda e: e.tensor_reduce(self.rc1b[:, :], self.mptm[:, :],
                                           AX, ALU.min), wait=[u4],
                 hard=True)
        if sc == 0:
            last = p.v(lambda e, cl=cl: e.tensor_copy(
                self.minP[:, cl:cl + 1], self.rc1b[:, :]), wait=[u5],
                inc=True, hard=True)
        else:
            last = p.v(lambda e, cl=cl: e.tensor_tensor(
                self.minP[:, cl:cl + 1], self.minP[:, cl:cl + 1],
                self.rc1b[:, :], ALU.min), wait=[u5], inc=True, hard=True)
        self._mp_war = [last]

    def _comb_job(self, b):
        p = self.p
        bs = slice(b * 8, b * 8 + 8)
        bp = slice(b * 8, b * 8 + 8)
        m0a = p.v(lambda e, bp=bp: e.tensor_copy(self.t8a[:, :],
                                                  self.maxP[:, bp]),
                   wait=self._mp_war + [self.tok_gb], hard=True)
        m0b = p.v(lambda e, bp=bp: e.tensor_copy(self.t8b[:, :],
                                                  self.minP[:, bp]),
                   wait=[m0a], hard=True)
        m1 = p.v(lambda e, bs=bs: e.tensor_mul(
            self.t8a[:, :], self.t8a[:, :], self.onepcb[:, bs]),
            wait=[m0b], hard=True)
        m2 = p.v(lambda e, bs=bs: e.tensor_mul(
            self.t8b[:, :], self.t8b[:, :], self.onepcb[:, bs]),
            wait=[m1], hard=True)
        m3 = p.v(lambda e: e.tensor_max(self.t8a[:, :], self.t8a[:, :],
                                        self.t8b[:, :]), wait=[m1, m2],
                 hard=True)
        m4 = p.v(lambda e, bs=bs: e.tensor_add(self.t8a[:, :], self.t8a[:, :],
                                               self.sbcol[:, bs]),
                 wait=[m3], hard=True)
        m5 = p.v(lambda e: e.tensor_reduce(self.rc1[:, :], self.t8a[:, :],
                                           AX, ALU.max), wait=[m4], hard=True)
        ms = p.v(lambda e: e.tensor_tensor(self.mx1[:, :], self.mx1[:, :],
                                           self.rc1[:, :], ALU.max),
                 wait=[self.tok_mx0, m5], inc=True, hard=True)
        self.mseq.append(ms)

    def p1_all(self):
        p = self.p
        p1 = ExitStack()
        s1 = lambda n, s, dt: p1.enter_context(self.nc.sbuf_tensor(n, s, dt))
        self.h1 = s1("h1", [128, NE * SCT], BF16)
        self.qsb = s1("qsb", [128, NE * SCT], BF16)
        self.ksb = s1("ksb", [128, NE * SCT], BF16)
        self.vsb = s1("vsb", [128, NCC * 1024], BF16)
        self.ktok = s1("ktok", [128, 1024], BF16)
        self.msb = s1("msb", [128, 2048], BF16)
        self.ysb = s1("ysb", [128, 1024], BF16)
        self.yT = s1("yT", [128, NE * SCT], BF16)
        self.Ssb = s1("Ssb", [128, 512], F32)
        self.zsb = s1("zsb", [128, 8], F32)
        self.Szb = s1("Szb", [128, 520], BF16)
        self.Rtok = s1("Rtok", [128, 16], F32)
        self.Rtmp = s1("Rtmp", [128, 16], F32)
        self.fm1 = s1("fm1", [128, SCT], BF16)
        self.Rbc = s1("Rbc", [128, SCT], BF16)
        self.Bbc = s1("Bbc", [128, SCT], BF16)

        self.sc_res = {}
        self.mseq = []
        self.tok_rows2 = {0: [], 1: []}
        self._mp_war = []
        self._mp_rb = []
        full = self.part == "wo" and self.upto in ("p2", "p3", "p4")
        self.resid_toks = {0: [], 1: []}
        self._h1_war = []
        self._rb_war = []
        self._h1_pending = None
        self._prev_zmm = None
        self._prev_e3 = None
        self._prev_lastYmm = None
        self._prev_ytr = None
        self._prev_e2 = None
        self._prev_e3 = None

        for b in range(self.nb):
            self.row_toks = list(self.tok_rows1[b]) + \
                (self.p0_done if b == 0 else [])
            zw = [self._prev_zmm] if self._prev_zmm else [self.tok_mx20]
            if b > 0:
                zw += [self.state_tok, self.sadd_tok, self.zadd_tok]
            else:
                zw += self.p0_done
            z1 = p.g(lambda e: e.memset(self.Ssb[:, :], 0.0),
                     wait=zw, inc=True, hard=True)
            z2 = p.g(lambda e: e.memset(self.zsb[:, :], 0.0), inc=True, hard=True)
            z3 = p.g(lambda e: e.memset(self.Szb[:, :], 0.0), inc=True, hard=True)
            self.state_tok = z3
            self.sadd_tok = z1
            self.zadd_tok = z2
            for sc in range(self.nsc):
                self.superchunk(b, sc)
                if full and self.nsc == NSC_B:
                    self.fillers_hi.append(
                        (6.0, lambda b=b, sc=sc: self._sp2_job(b, sc)))
                    self.fillers_hi.append(
                        (1.0, lambda b=b, sc=sc: self._mp_load(b, sc)))
                    for je in range(NE):
                        self.fillers_hi.append(
                            (2.0, lambda b=b, sc=sc, je=je:
                             self._mp_je(b, sc, je)))
                    if sc == NSC_B - 1:
                        self.fillers_hi.append(
                            (2.0, lambda b=b: self._comb_job(b)))
                self._pop_fillers(4.0)
            if self.tap in ("x1", "p1dbg") and b == 0:
                p.dma("sync", self.mktap("x1", [128, 2048], BF16), self.XT[:, 0:2048],
                      wait=self.resid_toks[0][-8:])
        self.p1_done = self.resid_toks[self.nb - 1][-8:]
        self._flush_fillers()
        p1.close()

    def h1_block(self, b, sc):
        p = self.p
        g0 = b * T + sc * SCT
        if b and self.part == "wo" and self.nsc == NSC_B:
            assert len(self.tok_rows1[b]) == 2 * NSC_B
        row_toks = list(self.tok_rows1[b]) + (self.p0_done if b == 0 else [])
        rb1 = p.dma("sync", self.Rbc[:, :],
                    self.rows_h[2 * b:2 * b + 1,
                                sc * SCT:(sc + 1) * SCT].broadcast_to([128, SCT]),
                    wait=row_toks + self._rb_war, sem="drb")
        rb2 = p.dma("sync", self.Bbc[:, :],
                    self.rows_h[2 * b + 1:2 * b + 2,
                                sc * SCT:(sc + 1) * SCT].broadcast_to([128, SCT]),
                    sem="drb")
        h1toks = []
        for je in range(NE):
            xs = self.XT[:, je * TPC + g0: je * TPC + g0 + SCT]
            hs = self.h1[:, je * SCT:(je + 1) * SCT]
            wh = [rb1, rb2, self.xt_toks[g0 // 1024], self.tok_h1mod]
            wh += self._h1_war
            p.v(lambda e, xs=xs, hs=hs: e.tensor_mul(hs, xs, self.Rbc[:, :]),
                wait=wh)
            p.v(lambda e, hs=hs: e.tensor_add(hs, hs, self.Bbc[:, :]))
            h1toks.append(p.v(lambda e, hs=hs, je=je, b=b: e.tensor_scalar(
                hs, hs,
                self.onepca[:, b * 8 + je:b * 8 + je + 1],
                self.sacol[:, b * 8 + je:b * 8 + je + 1], ALU.mult, ALU.add),
                inc=True, hard=True))
        self._rb_war = [h1toks[-1]]
        return h1toks

    def superchunk(self, b, sc):
        p = self.p
        g0 = b * T + sc * SCT
        h1toks = self._h1_pending
        self._h1_pending = None
        if h1toks is None:
            h1toks = self.h1_block(b, sc)

        if self.part == "ln":
            self.resid_toks[b] += h1toks
            return
        projQK = []
        for wi, woff in ((0, 0), (1, 8192)):
            for i in range(NE):
                k = wi * 8 + i
                half = k % 2
                bank = self.A01[:, half * 512:half * 512 + 512]
                mm = None
                for je in range(NE):
                    w = [h1toks[je], self.wtoks[wi]]
                    if je == 0:
                        if len(projQK) >= 2:
                            w.append(projQK[-2])
                        else:
                            w.append(self.bank["A01"])
                    mm = p.t(lambda e, woff=woff, i=i, je=je, bank=bank: e.matmul(
                        bank, self.WA[:, woff + je * 1024 + i * 128:
                                      woff + je * 1024 + (i + 1) * 128],
                        self.h1[:, je * SCT:(je + 1) * SCT],
                        start=(je == 0), stop=(je == NE - 1)),
                        wait=w, inc=(je == NE - 1))
                e2 = p.a(lambda e, bank=bank: e.activation(
                    self.fm1[:, :], bank, AF.Exp),
                    wait=[mm, self._prev_e3], inc=True, hard=True)
                ec = p.v(lambda e: e.tensor_scalar(
                    self.fm1[:, :], self.fm1[:, :], 1.0, None, ALU.min),
                    wait=[e2], hard=True)
                dst = self.qsb if wi == 0 else self.ksb
                e3 = p.v(lambda e, bank=bank, dst=dst, i=i:
                         e.scalar_tensor_tensor(
                             dst[:, i * SCT:(i + 1) * SCT], bank, 1.0,
                             self.fm1[:, :], ALU.add, ALU.max),
                         wait=[ec], inc=True, hard=True)
                self._prev_e3 = e3
                projQK.append(e3)
        self.bank["A01"] = projQK[-1]

        vtoks = []
        for cci in range(NCC):
            for half in range(2):
                bank = self.YB[:, half * 512:half * 512 + 512]
                mm = None
                for je in range(NE):
                    w = [h1toks[je], self.wtoks[2]]
                    if je == 0:
                        if len(vtoks) >= 2:
                            w.append(vtoks[-2])
                        else:
                            w.append(self.bank["YB"])
                    mm = p.t(lambda e, cci=cci, half=half, je=je, bank=bank:
                             e.matmul(
                        bank,
                        self.h1[:, je * SCT + cci * CC: je * SCT + cci * CC + CC],
                        self.WA[:, 16384 + je * 1024 + half * 512:
                                16384 + je * 1024 + (half + 1) * 512],
                        start=(je == 0), stop=(je == NE - 1)),
                        wait=w, inc=(je == NE - 1))
                ev = p.v(lambda e, cci=cci, half=half, bank=bank: e.tensor_copy(
                    self.vsb[:, cci * 1024 + half * 512:
                             cci * 1024 + (half + 1) * 512], bank),
                    wait=[mm], inc=True, hard=True)
                vtoks.append(ev)

        self._h1_war = [projQK[-1], vtoks[-1]]
        if self.tap in ("sc0", "p1dbg") and b == 0 and sc == 0:
            p.dma("sync", self.mktap("h1", [128, NE * SCT], BF16), self.h1[:, :],
                  wait=h1toks)
            p.dma("sync", self.mktap("q", [128, NE * SCT], BF16), self.qsb[:, :],
                  wait=projQK[:8])
            p.dma("sync", self.mktap("k", [128, NE * SCT], BF16), self.ksb[:, :],
                  wait=projQK[8:])
            p.dma("sync", self.mktap("v", [128, NCC * 1024], BF16), self.vsb[:, :],
                  wait=vtoks)

        if self.part == "proj":
            self.resid_toks[b] += [projQK[-1], vtoks[-1]]
            self.bank["YB"] = vtoks[-1]
            return
        ymult_tok = None
        yev_toks = []
        for cci in range(self.ncc):
            ktr = None
            for jd in range(NE):
                w = [projQK[8 + jd], self.tok_ib]
                if jd == 0:
                    w.append(self.bank["TB"])
                ktr = p.t(lambda e, jd=jd, cci=cci: e.transpose(
                    self.TB[:, jd * 128:(jd + 1) * 128],
                    self.ksb[:, jd * SCT + cci * CC: jd * SCT + cci * CC + CC],
                    self.identb[:, :]), wait=w, inc=(jd == NE - 1))
            ktk = p.v(lambda e: e.tensor_copy(self.ktok[:, :], self.TB[:, :]),
                      wait=[ktr, self._prev_zmm], inc=True, hard=True)
            if self.part == "kt":
                self.state_tok = ktk
                self.bank["TB"] = ktk
                continue
            if self.part in ("ata", "atb", "atc", "atd"):
                mm = None
                if self.part == "ata":
                    plan = [(h_, 0, h_ // 2) for h_ in range(8)]
                elif self.part == "atc":
                    plan = [(h_, 64, h_ // 2) for h_ in range(8)]
                elif self.part == "atd":
                    plan = ([(h_, 0, h_) for h_ in range(8)] +
                            [(h_, 64, h_) for h_ in range(8)])
                else:
                    plan = [(h_, (h_ % 2) * 64, h_ // 2) for h_ in range(16)]
                nh = len(plan)
                for idx, (h_, pp, jd) in enumerate(plan):
                    hh = h_ % 8
                    w = list(projQK) + ([self.bank["A01"]] if idx == 0 else [])
                    mm = p.t(lambda e, pp=pp, jd=jd, hh=hh, cci=cci: e.matmul(
                        self.A01[:, hh * 128:(hh + 1) * 128],
                        self.ksb[pp:pp + 64,
                                 jd * SCT + cci * CC: jd * SCT + cci * CC + CC],
                        self.qsb[pp:pp + 64,
                                 jd * SCT + cci * CC: jd * SCT + cci * CC + CC],
                        start=True, stop=True), wait=w,
                        inc=(idx == nh - 1))
                ev = p.v(lambda e: e.tensor_copy(self.msb[:, 0:1024],
                                                 self.A01[:, :]),
                         wait=[mm], inc=True, hard=True)
                self.bank["A01"] = ev
                self.state_tok = ev
                self.bank["TB"] = ktk
                continue
            mev = []
            for r in range(2):
                mm = None
                pp = r * 64
                for jd in range(8):
                    w = []
                    if jd == 0:
                        w = list(projQK)
                        w.append(mev[-1] if mev else self.bank["A01"])
                    mm = p.t(lambda e, pp=pp, jd=jd, cci=cci: e.matmul(
                        self.A01[:, jd * 128:(jd + 1) * 128],
                        self.ksb[pp:pp + 64,
                                 jd * SCT + cci * CC: jd * SCT + cci * CC + CC],
                        self.qsb[pp:pp + 64,
                                 jd * SCT + cci * CC: jd * SCT + cci * CC + CC],
                        start=True, stop=True), wait=w, inc=(jd == 7))
                mc = p.v(lambda e, r=r: e.tensor_copy(
                    self.msb[:, r * 1024:(r + 1) * 1024], self.A01[:, :]),
                    wait=[mm, self._prev_lastYmm], inc=True, hard=True)
                me = p.g(lambda e, r=r: e.affine_select(
                    self.msb[:, r * 1024:(r + 1) * 1024],
                    self.msb[:, r * 1024:(r + 1) * 1024],
                    pattern=[[0, 8], [1, 128]], compare_op=ALU.is_ge,
                    fill=0.0, channel_multiplier=-1),
                    wait=[mc], inc=True, hard=True)
                mev.append(me)
            self.bank["A01"] = mev[-1]
            if self.part == "at":
                self.state_tok = mev[-1]
                self.bank["TB"] = ktk
                continue
            def mcol(h_):
                return (h_ % 2) * 1024 + (h_ // 2) * 128
            last_mm = None
            for h_ in range(16):
                pp = (h_ % 2) * 64
                jd = h_ // 2
                w = [mev[h_ % 2], vtoks[cci * 2 + h_ // 8], self.state_tok]
                if h_ == 0:
                    w.append(ymult_tok if ymult_tok is not None else vtoks[-1])
                    w.append(self.zadd_tok)
                    w.append(self.tok_ones)
                p.t(lambda e, h_=h_, cci=cci: e.matmul(
                    self.YB[:, h_ * 64:(h_ + 1) * 64],
                    self.msb[:, mcol(h_):mcol(h_) + 128],
                    self.vsb[:, cci * 1024 + h_ * 64: cci * 1024 + (h_ + 1) * 64],
                    start=True, stop=False), wait=w)
                p.t(lambda e, h_=h_: e.matmul(
                    self.DB[:, h_:h_ + 1], self.msb[:, mcol(h_):mcol(h_) + 128],
                    self.ones_col[:, :], start=True, stop=False))
                p.t(lambda e, h_=h_, pp=pp, jd=jd, cci=cci: e.matmul(
                    self.YB[:, h_ * 64:(h_ + 1) * 64],
                    self.qsb[pp:pp + 64,
                             jd * SCT + cci * CC: jd * SCT + cci * CC + CC],
                    self.Szb[pp:pp + 64, jd * 64:(jd + 1) * 64],
                    start=False, stop=True))
                last_mm = p.t(lambda e, h_=h_, pp=pp, jd=jd, cci=cci: e.matmul(
                    self.DB[:, h_:h_ + 1],
                    self.qsb[pp:pp + 64,
                             jd * SCT + cci * CC: jd * SCT + cci * CC + CC],
                    self.Szb[pp:pp + 64, 512 + jd:513 + jd],
                    start=False, stop=True), inc=(h_ == 15))
            self._prev_lastYmm = last_mm
            if self.part == "yden":
                ymult_tok = p.v(lambda e: e.tensor_copy(
                    self.ysb[:, :], self.YB[:, :]), wait=[last_mm],
                    inc=True, hard=True)
                self.state_tok = ymult_tok
                self.bank["TB"] = ktk
                continue
            d1 = p.v(lambda e: e.tensor_scalar(self.Rtmp[:, :], self.DB[:, 0:16],
                                               1e-6, None, ALU.add),
                     wait=[last_mm], inc=True, hard=True)
            d2 = p.v(lambda e: e.reciprocal(self.Rtok[:, :], self.Rtmp[:, :]),
                     wait=[d1], inc=True, hard=True)
            rr = self.Rtok.ap().unsqueeze(2).broadcast_to([128, 16, 64])
            ymult_tok = p.v(lambda e, rr=rr: e.tensor_tensor(
                self.ysb.ap().rearrange("p (h e2) -> p h e2", h=16),
                self.YB.ap().rearrange("p (h e2) -> p h e2", h=16),
                rr, ALU.mult), wait=[d2, last_mm, self._prev_ytr],
                inc=True, hard=True)
            ysum = ymult_tok
            if self.part == "ym":
                self.state_tok = ymult_tok
                self.bank["TB"] = ktk
                continue
            ytr = None
            for jd in range(NE):
                w = [ymult_tok, ktk] if jd == 0 else []
                ytr = p.t(lambda e, jd=jd: e.transpose(
                    self.TB[:, jd * 128:(jd + 1) * 128],
                    self.ysb[:, jd * 128:(jd + 1) * 128], self.identb[:, :]),
                    wait=w, inc=(jd == NE - 1))
            self._prev_ytr = ytr
            yev = p.v(lambda e, cci=cci: e.tensor_copy(
                self.yT.ap().rearrange("p (jd t) -> p jd t", jd=NE)[
                    :, :, cci * CC:(cci + 1) * CC],
                self.TB.ap().rearrange("p (jd t) -> p jd t", jd=NE)),
                wait=[ytr], inc=True, hard=True)
            yev_toks.append(yev)
            self.bank["TB"] = yev
            if self.part == "ytr":
                self.state_tok = yev
                continue
            zmm = None
            sadd = None
            for rnd in range(2):
                dsmm = None
                for jj in range(4):
                    j = rnd * 4 + jj
                    w = [ktk, vtoks[cci * 2 + j // 4]]
                    if j == 0:
                        w += [self.sadd_tok, self.bank["MS"], d1]
                    elif jj == 0:
                        w += [sadd]
                    dsmm = p.t(lambda e, j=j, jj=jj, cci=cci: e.matmul(
                        self.MS[:, jj * 128:(jj + 1) * 128],
                        self.ktok[:, j * 128:(j + 1) * 128],
                        self.vsb[:, cci * 1024 + j * 128:
                                 cci * 1024 + (j + 1) * 128],
                        start=True, stop=True), wait=w, inc=(jj == 3))
                    zmm = p.t(lambda e, j=j: e.matmul(
                        self.DB[:, 16 + j:17 + j],
                        self.ktok[:, j * 128:(j + 1) * 128],
                        self.ones_col[:, :], start=True, stop=True),
                        inc=(j == 7))
                off = rnd * 4 * 64
                sa_dst0 = self.Ssb.ap()[0:64, off:off + 256].rearrange(
                    "p (j q) -> p j q", j=4)
                sa_src0 = self.MS.ap()[0:64, :].rearrange(
                    "p (j q) -> p j q", j=4)[:, :, 0:64]
                sa_dst1 = self.Ssb.ap()[64:128, off:off + 256].rearrange(
                    "p (j q) -> p j q", j=4)
                sa_src1 = self.MS.ap()[64:128, :].rearrange(
                    "p (j q) -> p j q", j=4)[:, :, 64:128]
                p.v(lambda e, sa_dst0=sa_dst0, sa_src0=sa_src0: e.tensor_tensor(
                    sa_dst0, sa_dst0, sa_src0, ALU.add),
                    wait=[dsmm], hard=True)
                sadd = p.v(lambda e, sa_dst1=sa_dst1, sa_src1=sa_src1:
                           e.tensor_tensor(sa_dst1, sa_dst1, sa_src1, ALU.add),
                           inc=True, hard=True)
            self._prev_zmm = zmm
            self.bank["MS"] = sadd
            zadd = p.v(lambda e: e.tensor_add(self.zsb[:, :], self.zsb[:, :],
                                              self.DB[:, 16:24]),
                       wait=[self.state_tok, zmm], inc=True, hard=True)
            p.v(lambda e: e.tensor_copy(self.Szb[:, 0:512], self.Ssb[:, :]),
                wait=[sadd])
            sc2 = p.v(lambda e: e.tensor_copy(self.Szb[:, 512:520],
                                              self.zsb[:, :]),
                      wait=[zadd], inc=True, hard=True)
            self.sadd_tok = sadd
            self.zadd_tok = zadd
            self.state_tok = sc2
            self.bank["A01"] = mev[-1]

            if self.part == "wo" and self.nsc == NSC_B:
                if cci == 0:
                    nxt = b * self.nsc + sc + 1
                    if nxt < self.nb * self.nsc:
                        self._h1_pending = self.h1_block(nxt // NSC_B,
                                                         nxt % NSC_B)
                else:
                    self._pop_fillers(7.0)

            if self.tap in ("y0", "p1dbg") and b == 0 and sc == 0 and cci == 0:
                p.dma("sync", self.mktap("y0", [128, 1024], BF16), self.ysb[:, :],
                      wait=[ymult_tok])
                p.dma("sync", self.mktap("den0", [128, 16]), self.Rtmp[:, :],
                      wait=[d1])
                p.dma("sync", self.mktap("ktok0", [128, 1024], BF16), self.ktok[:, :],
                      wait=[ktk])
                p.dma("sync", self.mktap("S0", [128, 512]), self.Ssb[:, :],
                      wait=[sadd])

        if self.part in ("scan", "kt", "at", "ata", "atb", "atc", "atd", "yden", "ym", "ytr"):
            self.resid_toks[b] += [self.state_tok]
            self.bank["YB"] = ymult_tok if ymult_tok else vtoks[-1]
            return
        res8 = []
        for i in range(NE):
            half = i % 2
            bank = self.YB[:, half * 512:half * 512 + 512]
            mm = None
            for je in range(NE):
                w = [yev_toks[-1], self.wtoks[3]]
                if je == 0:
                    w.append(res8[-2] if len(res8) >= 2 else ymult_tok)
                mm = p.t(lambda e, i=i, je=je, bank=bank: e.matmul(
                    bank, self.WA[:, 24576 + je * 1024 + i * 128:
                                  24576 + je * 1024 + (i + 1) * 128],
                    self.yT[:, je * SCT:(je + 1) * SCT],
                    start=(je == 0), stop=(je == NE - 1)),
                    wait=w, inc=(je == NE - 1))
            res = p.v(lambda e, i=i, bank=bank, b=b, g0=g0: e.scalar_tensor_tensor(
                self.XT[:, i * TPC + g0: i * TPC + g0 + SCT], bank,
                self.gacol[:, b * 8 + i:b * 8 + i + 1],
                self.XT[:, i * TPC + g0: i * TPC + g0 + SCT],
                ALU.mult, ALU.add), wait=[mm, self.tok_ga],
                inc=True, hard=True)
            res8.append(res)
        self.resid_toks[b] += res8
        self.sc_res[(b, sc)] = res8
        self.bank["YB"] = res8[-1]

    def p2(self):
        p = self.p
        pd = self.p1_done
        p2 = ExitStack()
        s2 = lambda n, s_, dt: p2.enter_context(self.nc.sbuf_tensor(n, s_, dt))
        wqI1 = s2("wqI1", [128, D], I8)
        wqF1 = [s2("wqF0", [128, D], F32), s2("wqF1", [128, D], F32)]
        wqB1 = [s2("wqB0", [128, D], BF16), s2("wqB1", [128, D], BF16)]
        self._wqF2 = s2("wqF2", [128, FF], F32)
        self._wqI2 = s2("wqI2", [128, FF], I8)

        tokc = self.cross_max(self.mx1, list(self.mseq), 0)
        self.tok_s1 = self.scale_consts(tokc, 0)

        self._wq1 = dict(wqF=wqF1, wqI=[wqI1], wqB=wqB1,
                         buf_free=[list(pd), list(pd)],
                         st_free=[None, None], q3_free=[None], ai=0,
                         cast_store=False)
        for a in range(NF):
            st = self.wq_tile(self._wq1, self.w1_h, self.w1b_h, a, 1024,
                              self.sw1b)
        self.tok_w1q = st
        self.tok_w1q_all = [self._wq1["st_free"][0], self._wq1["st_free"][1]]
        self._p2_scope = p2
        f1 = p.v(lambda e: e.tensor_scalar(self.sw1b[:, :], self.sw1b[:, :],
                                           C127I, None, ALU.mult),
                 wait=[self.tok_w1q], inc=True, hard=True)
        self.tok_sw = [f1]

        self.w1t_toks = []
        for jd in range(NE):
            self.w1t_toks.append(p.dma(
                "sync", self.WA[:, jd * 4096:(jd + 1) * 4096],
                self.w1b_h[:, jd * 128:(jd + 1) * 128],
                wait=[self.tok_w1q] + pd, transpose=True, sem="dw1t"))

        q1 = p.v(lambda e: e.tensor_scalar(self.sc1c[:, :], self.sw1b[:, :],
                                           self.scl[:, 1:2], None, ALU.mult),
                 wait=[self.tok_s1, f1], inc=True, hard=True)
        q2 = p.v(lambda e: e.reciprocal(self.qt1[:, :], self.sc1c[:, :]),
                 wait=[q1], inc=True, hard=True)
        q3 = p.v(lambda e: e.tensor_mul(self.qt2[:, :], self.b1c[:, :],
                                        self.qt1[:, :]),
                 wait=[q2, self.tok_b1c], inc=True)
        q4 = p.v(lambda e: e.tensor_copy(self.qi32[:, :], self.qt2[:, :]),
                 wait=[q3], inc=True)
        q5 = p.v(lambda e: e.tensor_copy(self.qt2[:, :], self.qi32[:, :]),
                 wait=[q4], inc=True)
        self.tok_bq1 = p.v(lambda e: e.tensor_mul(self.bq1c[:, :], self.qt2[:, :],
                                                  self.sc1c[:, :]),
                           wait=[q5], inc=True, hard=True)
        a1 = p.v(lambda e: e.tensor_scalar(self.ascol[:, :], self.onepcb[:, :],
                                           self.scl[:, 0:1], None, ALU.mult),
                 wait=[self.tok_s1, self.tok_gb], inc=True, hard=True)
        self.tok_asb = p.v(lambda e: e.tensor_scalar(
            self.bscol[:, :], self.sbcol[:, :], self.scl[:, 0:1], None,
            ALU.mult), wait=[a1], inc=True, hard=True)
        self.p2_done = [self.tok_bq1, self.mseq[-1]]

        if self.tap == "ffn":
            p.dma("sync", self.mktap("scl", [128, 4]), self.scl[:, :],
                  wait=[self.tok_s1])
            p.dma("sync", self.mktap("sw1", [128, 32]), self.sw1b[:, :], wait=[f1])
            p.dma("sync", self.mktap("bq1", [128, 32]), self.bq1c[:, :],
                  wait=[self.tok_bq1])
            p.dma("sync", self.mktap("mx1", [128, 1]), self.mx1[:, :],
                  wait=[self.mseq[-1]])

    def wq_tile(self, wq, w_src, w_dst, a, ncol, swc):
        p = self.p
        par = wq["ai"] % len(wq["wqF"])
        ipar = wq["ai"] % len(wq["wqI"])
        wF = wq["wqF"][par]
        wI = wq["wqI"][ipar]
        ld = p.dma("sync", wF[:, 0:ncol], w_src[a * 128:(a + 1) * 128, :],
                   wait=wq["buf_free"][par], sem="dwq%d" % par)
        am = p.v(lambda e, ncol=ncol, wF=wF: e.tensor_reduce(
            self.rc1[:, :], wF[:, 0:ncol], AX, ALU.max,
            apply_absolute_value=True), wait=[ld], inc=True, hard=True)
        am2 = p.v(lambda e, swc=swc, a=a: e.tensor_scalar(
            swc[:, a:a + 1], self.rc1[:, :], 1e-8, None, ALU.max),
            wait=[am], inc=True, hard=True)
        rq0 = p.v(lambda e, swc=swc, a=a: e.reciprocal(
            self.rc1[:, :], swc[:, a:a + 1]), wait=[am2], hard=True)
        rq = p.v(lambda e: e.tensor_scalar(self.rc1[:, :], self.rc1[:, :],
                                           127.0, None, ALU.mult),
                 wait=[rq0], inc=True, hard=True)
        if wq["cast_store"]:
            q = p.v(lambda e, ncol=ncol, wF=wF, wI=wI: e.tensor_scalar(
                wI[:, 0:ncol], wF[:, 0:ncol], self.rc1[:, 0:1], None,
                ALU.mult), wait=[rq, wq["st_free"][0]], inc=True, hard=True)
            st = p.dma("gpsimd", w_dst[a * 128:(a + 1) * 128, :],
                       wI[:, 0:ncol], wait=[q], sem="dwsc")
            wq["st_free"][0] = st
        else:
            q = p.v(lambda e, ncol=ncol, wF=wF, wI=wI: e.tensor_scalar(
                wI[:, 0:ncol], wF[:, 0:ncol], self.rc1[:, 0:1], None,
                ALU.mult), wait=[rq, wq["q3_free"][0]], inc=True, hard=True)
            wB = wq["wqB"][par]
            q3 = p.a(lambda e, ncol=ncol, wB=wB, wI=wI: e.activation(
                wB[:, 0:ncol], wI[:, 0:ncol], AF.Copy),
                wait=[q, wq["st_free"][par]], inc=True, hard=True)
            wq["q3_free"][0] = q3
            st = p.dma("scalar", w_dst[a * 128:(a + 1) * 128, :],
                       wB[:, 0:ncol], wait=[q3], sem="dws%d" % par)
            wq["st_free"][par] = st
        wq["buf_free"][par] = [q]
        wq["ai"] += 1
        return st

    def cross_max(self, mxcol, deps, idx):
        p = self.p
        tt = p.t(lambda e, mxcol=mxcol: e.transpose(
            self.MS[0:1, 0:128], mxcol[:, :], self.identf[:, :]),
            wait=list(deps) + [self.tok_if, self.bank["MS"]], inc=True)
        e1 = p.v(lambda e: e.tensor_reduce(self.ccs[0:1, 0:1],
                                           self.MS[0:1, 0:128], AX, ALU.max),
                 wait=[tt, self.tok_ccs0], inc=True, hard=True)
        self.bank["MS"] = e1
        d1 = p.dma("gpsimd", self.cci_h[idx], self.ccs[0:1, :],
                   wait=[e1, self.tok_ccs0], sem="dcc")
        cc = p.g(lambda e, idx=idx: e.collective_compute(
            "AllReduce", ALU.max, replica_groups=[list(range(8))],
            ins=[self.cci_h[idx].opt()], outs=[self.cco_h[idx].opt()]),
            wait=[d1], inc=True, hard=True)
        d2 = p.dma("gpsimd", self.ccs[0:1, :], self.cco_h[idx], wait=[cc],
                   sem="dcc")
        return d2

    def scale_consts(self, tokc, pairidx):
        p = self.p
        c0, c1 = 2 * pairidx, 2 * pairidx + 1
        e1 = p.v(lambda e: e.tensor_single_scalar(
            self.ccs[0:1, 1:2], self.ccs[0:1, 0:1], 1e-8, ALU.max),
            wait=[tokc], inc=True, hard=True)
        e2 = p.v(lambda e: e.tensor_single_scalar(
            self.ccs[0:1, 2:3], self.ccs[0:1, 1:2], C255I, ALU.mult),
            wait=[e1], inc=True, hard=True)
        e3 = p.v(lambda e: e.reciprocal(self.ccs[0:1, 3:4], self.ccs[0:1, 2:3]),
                 wait=[e2], inc=True, hard=True)
        mm1 = p.t(lambda e: e.matmul(self.MS[:, 0:1], self.ones_rowf[0:1, :],
                                     self.ccs[0:1, 2:3], start=True, stop=True),
                  wait=[e2, e3, self.tok_onesrf, self.bank["MS"]], inc=True)
        mm2 = p.t(lambda e: e.matmul(self.MS[:, 1:2], self.ones_rowf[0:1, :],
                                     self.ccs[0:1, 3:4], start=True, stop=True),
                  inc=True)
        ev = p.v(lambda e, c0=c0, c1=c1: e.tensor_copy(
            self.scl[:, c1:c1 + 1], self.MS[:, 0:1]),
            wait=[mm1, mm2], inc=True, hard=True)
        ev2 = p.v(lambda e, c0=c0: e.tensor_copy(self.scl[:, c0:c0 + 1],
                                                 self.MS[:, 1:2]),
                  wait=[ev], inc=True, hard=True)
        self.bank["MS"] = ev2
        return ev2

    def p3(self):
        p = self.p
        nc = self.nc
        p3 = ExitStack()
        s3 = lambda n, s, dt: p3.enter_context(nc.sbuf_tensor(n, s, dt))
        Rb = s3("Rb3", [128, SCT], BF16)
        Bb = s3("Bb3", [128, SCT], BF16)
        tmf = s3("tmf3", [128, SCT], F32)
        hu8 = s3("hu8", [128, SCT], U8)
        h2qb = [s3("h2q0", [128, NE * SCT], BF16),
                s3("h2q1", [128, NE * SCT], BF16)]
        mst = [s3("mst%d" % i, [128, SCT], F32) for i in range(2)] + \
            list(self.mstg)

        pd = self.p2_done
        mst_free = [None, None, None, None]
        self._last_gmax = self.tok_mx20
        self._last_middma = None
        rows2 = self.tok_rows2[0] + self.tok_rows2[1]
        NCH = BPC * NSC_B
        h2q_toks = [[None] * NE, [None] * NE]
        self._p3_mm_par = [None, None]
        self._h2q_last = [None, None]
        self._rbprev = None

        def emit_h2q(ci):
            b, sc = divmod(ci, NSC_B)
            g0 = b * T + sc * SCT
            h2q = h2qb[ci % 2]
            prev_par_mm = self._p3_mm_par[ci % 2]
            w_rb = rows2 + pd
            if self._rbprev is not None:
                w_rb.append(self._rbprev)
            else:
                w_rb += self._mp_war
            rb1 = p.dma("sync", Rb[:, :],
                        self.rows_h[4 + 2 * b:5 + 2 * b,
                                    sc * SCT:(sc + 1) * SCT].broadcast_to(
                                        [128, SCT]), wait=w_rb, sem="drb")
            rb2 = p.dma("sync", Bb[:, :],
                        self.rows_h[5 + 2 * b:6 + 2 * b,
                                    sc * SCT:(sc + 1) * SCT].broadcast_to(
                                        [128, SCT]), sem="drb")
            for je in range(NE):
                xs = self.XT[:, je * TPC + g0: je * TPC + g0 + SCT]
                w = [rb1, rb2] + pd
                if prev_par_mm is not None:
                    w.append(prev_par_mm)
                v1 = p.v(lambda e, xs=xs: e.tensor_mul(tmf[:, :], xs,
                                                       Rb[:, :]),
                         wait=w, hard=True)
                v2 = p.v(lambda e: e.tensor_add(tmf[:, :], tmf[:, :],
                                                Bb[:, :]), wait=[v1],
                         hard=True)
                cl = b * 8 + je
                vq = p.v(lambda e, cl=cl: e.tensor_scalar(
                    hu8[:, :], tmf[:, :],
                    self.ascol[:, cl:cl + 1], self.bscol[:, cl:cl + 1],
                    ALU.mult, ALU.add), wait=[self.tok_asb, v2], hard=True)
                h2q_toks[ci % 2][je] = p.v(
                    lambda e, je=je, h2q=h2q: e.tensor_copy(
                        h2q[:, je * SCT:(je + 1) * SCT], hu8[:, :]),
                    wait=[vq], inc=True, hard=True)
            self._h2q_last[ci % 2] = h2q_toks[ci % 2][NE - 1]
            self._rbprev = h2q_toks[ci % 2][NE - 1]

        emit_h2q(0)
        self._wq2 = dict(wqF=[self._wqF2], wqI=[self._wqI2],
                         buf_free=[self._wq1["buf_free"][0] +
                                   self._wq1["buf_free"][1]],
                         st_free=[self._wq1["q3_free"][0]], ai=0,
                         cast_store=True)
        mm = None
        prev_ep = [None, None]
        for ci in range(NCH):
            if ci + 1 < NCH:
                emit_h2q(ci + 1)
            b, sc = divmod(ci, NSC_B)
            g0 = b * T + sc * SCT
            h2q = h2qb[ci % 2]
            for fi in range(NF):
                half = fi % 2
                bank = self.A01[:, half * 512:half * 512 + 512]
                mm = None
                for jd in range(NE):
                    w = [h2q_toks[ci % 2][jd], self.w1t_toks[jd]]
                    if jd == 0:
                        w.append(prev_ep[half] if prev_ep[half]
                                 else self.bank["A01"])
                    mm = p.t(lambda e, fi=fi, jd=jd, bank=bank, h2q=h2q:
                             e.matmul(
                        bank, self.WA[:, jd * 4096 + fi * 128:
                                      jd * 4096 + (fi + 1) * 128],
                        h2q[:, jd * SCT:(jd + 1) * SCT],
                        start=(jd == 0), stop=(jd == NE - 1)),
                        wait=w, inc=(jd == NE - 1))
                stg = mst[fi % 4]
                w_ep = [mm, self.tok_bq1] + (mst_free[fi % 4] or [])
                ep = p.a(lambda e, fi=fi, bank=bank, stg=stg: e.activation(
                    stg[:, :], bank, AF.Relu,
                    bias=self.bq1c[:, fi:fi + 1],
                    scale=self.sc1c[:, fi:fi + 1]),
                    wait=w_ep, inc=True, hard=True)
                prev_ep[half] = ep
                dm = p.dma("scalar",
                           self.mid_h[fi * 128:(fi + 1) * 128, g0:g0 + SCT],
                           stg[:, :], wait=[ep], sem="dms%d" % (fi % 2))
                gm = p.v(lambda e, stg=stg: e.tensor_reduce(
                    self.gmx[:, :], stg[:, :], AX, ALU.max),
                    wait=[ep], inc=True, hard=True)
                gm2 = p.v(lambda e: e.tensor_tensor(
                    self.mx2[:, :], self.mx2[:, :], self.gmx[:, :],
                    ALU.max), wait=[self.tok_mx20, gm], inc=True, hard=True)
                self._last_gmax = gm2
                self._last_middma = dm
                mst_free[fi % 4] = [dm, gm2]
            if ci < NE:
                self.tok_w2q = self.wq_tile(self._wq2, self.w2_h,
                                            self.w2b_h, ci, 4096, self.sw2b)
            self.bank["A01"] = prev_ep[1]
            self._p3_mm_par[ci % 2] = mm
        self._p3_last_mm = mm
        self._p3_last_mm = mm
        f2 = p.v(lambda e: e.tensor_scalar(self.sw2b[:, :], self.sw2b[:, :],
                                           C127I, None, ALU.mult),
                 wait=[self.tok_w2q], inc=True, hard=True)
        self.tok_sw = self.tok_sw + [f2]

        tokc = self.cross_max(self.mx2, [self._last_gmax, self._last_middma], 1)
        self.tok_s2 = self.scale_consts(tokc, 1)
        q1 = p.v(lambda e: e.tensor_scalar(self.t8a[:, :], self.sw2b[:, :],
                                           self.scl[:, 3:4], None, ALU.mult),
                 wait=[self.tok_s2] + self.tok_sw, inc=True, hard=True)
        q2 = p.v(lambda e: e.reciprocal(self.t8b[:, :], self.t8a[:, :]),
                 wait=[q1], inc=True, hard=True)
        q3 = p.v(lambda e: e.tensor_mul(self.qt1[:, 0:8], self.b2c[:, :],
                                        self.t8b[:, :]),
                 wait=[q2, self.tok_b2c], inc=True)
        q4 = p.v(lambda e: e.tensor_copy(self.qi32[:, 0:8], self.qt1[:, 0:8]),
                 wait=[q3], inc=True)
        q5 = p.v(lambda e: e.tensor_copy(self.qt1[:, 0:8], self.qi32[:, 0:8]),
                 wait=[q4], inc=True)
        bq2 = p.v(lambda e: e.tensor_mul(self.qt2[:, 0:8], self.qt1[:, 0:8],
                                         self.t8a[:, :]),
                  wait=[q5], inc=True, hard=True)
        gtoks = []
        for b in range(BPC):
            bs = slice(b * 8, b * 8 + 8)
            gtoks.append(p.v(lambda e, bs=bs: e.tensor_mul(
                self.gsc2[:, bs], self.t8a[:, :], self.gbcol[:, bs]),
                wait=[q1, self.tok_gates], inc=True, hard=True))
            gtoks.append(p.v(lambda e, bs=bs: e.tensor_mul(
                self.gbq2[:, bs], self.qt2[:, 0:8], self.gbcol[:, bs]),
                wait=[bq2], inc=True, hard=True))
        self.tok_g2 = gtoks
        self.p3_done = [self._p3_last_mm, self._last_middma, self._last_gmax,
                        gtoks[-1], self.tok_w2q]
        p3.close()
        self._p2_scope.close()

    def p4(self):
        p = self.p
        nc = self.nc
        p4 = ExitStack()
        s4 = lambda n, s, dt: p4.enter_context(nc.sbuf_tensor(n, s, dt))
        WB = self.WA
        mq = s4("mq", [128, NF * 512], BF16)
        mld = [s4("mld0", [128, SCT], F32), s4("mld1", [128, SCT], F32)]
        mtf = s4("mtf", [128, SCT], F32)
        mu8 = s4("mu8", [128, SCT], U8)
        xps = s4("xps", [128, SCT], BF16)
        ost = s4("ost", [128, NE * SCT], BF16)
        ost2 = s4("ost2", [128, 1024], F32)

        pd = self.p3_done
        w2t = []
        for jf in range(NF):
            w2t.append(p.dma(
                "sync", WB[:, jf * 1024:(jf + 1) * 1024],
                self.w2b_h[:, jf * 128:(jf + 1) * 128],
                wait=[self.tok_w2q] + pd, transpose=True, sem="dwb"))

        ev_toks = [None, None, None, None]
        tr_ev = None
        prev_out_dma = None
        last_out = None
        banks4 = [self.A01[:, 0:512], self.A01[:, 512:1024],
                  self.YB[:, 0:512], self.YB[:, 512:1024]]
        g2_jf = [None] * NF
        self._p4q = {}
        self._p4_lm = 0

        def quant_tile(jf, g0):
            buf = mld[self._p4_lm % 2]
            self._p4_lm += 1
            w = [self._last_middma] + pd
            if self._p4_mq2ago[0] is not None:
                w.append(self._p4_mq2ago[0])
            lm = p.dma("sync", buf[:, :],
                       self.mid_h[jf * 128:(jf + 1) * 128, g0:g0 + SCT],
                       wait=w, sem="dml%d" % (self._p4_lm % 2))
            n1 = p.v(lambda e, buf=buf: e.tensor_scalar(
                mtf[:, :], buf[:, :], self.scl[:, 2:3], None, ALU.mult),
                wait=[lm, self.tok_s2], hard=True)
            n2 = p.v(lambda e: e.tensor_copy(mu8[:, :], mtf[:, :]),
                     wait=[n1], hard=True)
            wq = [n2]
            if g2_jf[jf] is not None:
                wq.append(g2_jf[jf])
            tok = p.v(lambda e, jf=jf: e.tensor_copy(
                mq[:, jf * 512:(jf + 1) * 512], mu8[:, :]),
                wait=wq, inc=True, hard=True)
            self._p4_mq2ago = [self._p4_mq2ago[1], tok]
            return tok

        self._p4_mq2ago = [None, None]
        for b in range(BPC):
            for sc in range(NSC_B):
                g0 = b * T + sc * SCT
                mq_toks = {}
                g1_mm = None
                for jf in range(NF):
                    if jf in self._p4q:
                        mq_toks[jf] = self._p4q.pop(jf)
                    else:
                        mq_toks[jf] = quant_tile(jf, g0)
                    for i in range(4):
                        w2 = [mq_toks[jf], w2t[jf]]
                        if jf == 0:
                            w2.append(ev_toks[i] if ev_toks[i]
                                      else self.bank["YB"])
                            w2.append(self.bank["A01"])
                        g1_mm = p.t(lambda e, i=i, jf=jf: e.matmul(
                            banks4[i], WB[:, jf * 1024 + i * 128:
                                          jf * 1024 + (i + 1) * 128],
                            mq[:, jf * 512:(jf + 1) * 512],
                            start=(jf == 0), stop=(jf == NF - 1)),
                            wait=w2, inc=(jf == NF - 1 and i == 3))
                for i in range(4):
                    xp = p.v(lambda e, i=i, b=b, g0=g0: e.tensor_scalar(
                        xps[:, :], self.XT[:, i * TPC + g0: i * TPC + g0 + SCT],
                        self.gbq2[:, b * 8 + i:b * 8 + i + 1], None, ALU.add),
                        wait=self.tok_g2 + [g1_mm], hard=True)
                    ev_toks[i] = p.v(lambda e, i=i, b=b: e.scalar_tensor_tensor(
                        ost[:, i * SCT:(i + 1) * SCT], banks4[i],
                        self.gsc2[:, b * 8 + i:b * 8 + i + 1],
                        xps[:, :], ALU.mult, ALU.add),
                        wait=[xp], inc=True, hard=True)
                g2_mm = None
                for jf in range(NF):
                    for i in range(4):
                        w2 = [mq_toks[jf]]
                        if jf == 0:
                            w2.append(ev_toks[i])
                        g2_mm = p.t(lambda e, i=i, jf=jf: e.matmul(
                            banks4[i], WB[:, jf * 1024 + (i + 4) * 128:
                                          jf * 1024 + (i + 5) * 128],
                            mq[:, jf * 512:(jf + 1) * 512],
                            start=(jf == 0), stop=(jf == NF - 1)),
                            wait=w2, inc=(i == 3))
                    g2_jf[jf] = g2_mm
                for i in range(4):
                    xp = p.v(lambda e, i=i, b=b, g0=g0: e.tensor_scalar(
                        xps[:, :],
                        self.XT[:, (i + 4) * TPC + g0: (i + 4) * TPC + g0 + SCT],
                        self.gbq2[:, b * 8 + i + 4:b * 8 + i + 5], None,
                        ALU.add), wait=self.tok_g2 + [g2_mm], hard=True)
                    ev_toks[i] = p.v(lambda e, i=i, b=b:
                                     e.scalar_tensor_tensor(
                        ost[:, (i + 4) * SCT:(i + 5) * SCT], banks4[i],
                        self.gsc2[:, b * 8 + i + 4:b * 8 + i + 5],
                        xps[:, :], ALU.mult, ALU.add),
                        wait=[xp], inc=True, hard=True)
                last_ev = ev_toks[3]
                nxt = b * NSC_B + sc + 1
                if nxt < BPC * NSC_B:
                    ng0 = (nxt // NSC_B) * T + (nxt % NSC_B) * SCT
                    for jf in range(6):
                        self._p4q[jf] = quant_tile(jf, ng0)
                for cci in range(NCC):
                    tr = None
                    for i in range(NE):
                        w = [last_ev, self.tok_ib]
                        if i == 0 and tr_ev is not None:
                            w.append(tr_ev)
                        tr = p.t(lambda e, i=i, cci=cci: e.transpose(
                            self.TB[:, i * 128:(i + 1) * 128],
                            ost[:, i * SCT + cci * CC: i * SCT + cci * CC + CC],
                            self.identb[:, :]), wait=w, inc=(i == NE - 1))
                    tr_ev = p.v(lambda e: e.tensor_copy(ost2[:, :],
                                                        self.TB[:, :]),
                                wait=[tr, prev_out_dma], inc=True, hard=True)
                    t0 = sc * SCT + cci * CC
                    last_out = p.dma("sync", self.out_h[b, t0:t0 + CC, :],
                                     ost2[:, :], wait=[tr_ev], sem="dout")
                    prev_out_dma = last_out
                self.bank["TB"] = tr_ev
                self.bank["A01"] = last_ev
                self.bank["YB"] = last_ev
        self.final_out = last_out
        p4.close()

    def finale(self):
        p = self.p
        toks = [(sem, p.semc[sem], True) for sem in sorted(p.dma_sems)
                if p.semc[sem]]
        p.wait("sync", toks)


def build(upto="p4", tap=None, nb=BPC, nsc=NSC_B, part="wo", ncc=NCC):
    mk = MK(upto=upto, tap=tap, nb=nb, nsc=nsc, part=part, ncc=ncc)
    nc = mk.build()
    return nc, mk.taps


def kernel(**inputs):
    nc, _ = build()
    in_maps = []
    x = np.ascontiguousarray(inputs["L0_attn_x"])
    for k in range(8):
        in_maps.append({
            "x": x[2 * k:2 * k + 2],
            "wq": inputs["L0_attn_Wq"], "wk": inputs["L0_attn_Wk"],
            "wv": inputs["L0_attn_Wv"], "wo": inputs["L0_attn_Wo"],
            "w1": inputs["L0_ffn_w1"], "b1": inputs["L0_ffn_b1"],
            "w2": inputs["L0_ffn_w2"], "b2": inputs["L0_ffn_b2"],
            "c": inputs["L0_c"][2 * k:2 * k + 2],
            "wmod": inputs["L0_wmod"],
        })
    res = run_bass_kernel_spmd(nc, in_maps, core_ids=list(range(8)))
    out = np.empty((B, T, D), np.float32)
    for k in range(8):
        out[2 * k:2 * k + 2] = res.results[k]["out"]
    return out


# version 25
# speedup vs baseline: 1.0046x; 1.0046x over previous
import numpy as np
from contextlib import ExitStack
import concourse.bass as bass
import concourse.mybir as mybir
from concourse.bass_utils import run_bass_kernel_spmd

F32 = mybir.dt.float32
BF16 = mybir.dt.bfloat16
U8 = mybir.dt.uint8
I8 = mybir.dt.int8
I32 = mybir.dt.int32
AF = mybir.ActivationFunctionType
ALU = mybir.AluOpType
AX = mybir.AxisListType.X

B, T, D, H, HD, FF = 16, 2048, 1024, 16, 64, 4096
BPC = 2
TPC = BPC * T
SCT = 512
NSC_B = T // SCT
CC = 128
NCC = SCT // CC
NE = D // 128
NF = FF // 128
C1_1024 = float(np.float32(1.0 / 1024.0))
C255I = float(np.float32(1.0 / 255.0))
C127I = float(np.float32(1.0 / 127.0))

ENGINES = ("sync", "vector", "scalar", "tensor", "gpsimd")


class Prog:
    def __init__(self, nc, ctx):
        self.nc = nc
        self.steps = {e: [] for e in ENGINES}
        self.semh = {}
        self.semc = {}
        for name in ("sv", "sa", "st", "sg"):
            self.semh[name] = ctx.enter_context(nc.semaphore(name))
            self.semc[name] = 0
        self.dma_sems = set()
        self._ctx = ctx
        self.last_tok = {"vector": None, "scalar": None, "gpsimd": None}
        self.eng_sem = {"vector": "sv", "scalar": "sa", "tensor": "st",
                        "gpsimd": "sg"}
        self.waited = {e: {} for e in ENGINES}

    def _op(self, eng, fn, wait, inc, inc_n=1, hard=False):
        wait = list(wait)
        if eng in self.last_tok and fn is not None:
            wait.append(self.last_tok[eng])
        waits = []
        for tok in wait:
            if tok is None:
                continue
            (sname, v, thard) = tok
            if sname in self.dma_sems:
                v = self.semc[sname]
            if v <= 0:
                continue
            if self.waited[eng].get(sname, 0) >= v:
                continue
            self.waited[eng][sname] = v
            waits.append((self.semh[sname], v))
        tok = None
        incs = []
        if inc is not None:
            self.semc[inc] += inc_n
            tok = (inc, self.semc[inc], hard)
            incs = [(self.semh[inc], inc_n)]
        self.steps[eng].append((waits, fn, incs))
        if eng in self.last_tok and fn is not None and tok is not None:
            self.last_tok[eng] = tok
        return tok

    def t(self, fn, wait=(), inc=False):
        return self._op("tensor", fn, wait, "st" if inc else None)

    def v(self, fn, wait=(), inc=True, hard=False):
        return self._op("vector", fn, wait, "sv" if inc else None, hard=hard)

    def a(self, fn, wait=(), inc=True, hard=False):
        return self._op("scalar", fn, wait, "sa" if inc else None, hard=hard)

    def g(self, fn, wait=(), inc=True, hard=False):
        return self._op("gpsimd", fn, wait, "sg" if inc else None, hard=hard)

    def dma(self, eng, out, in_, wait=(), transpose=False, sem="dmisc"):
        if sem not in self.semh:
            self.semh[sem] = self._ctx.enter_context(self.nc.semaphore(sem))
            self.semc[sem] = 0
            self.dma_sems.add(sem)
        if transpose:
            fn = (lambda e, o=out, i=in_: e.dma_start(out=o, in_=i, transpose=True))
        else:
            fn = (lambda e, o=out, i=in_: e.dma_start(out=o, in_=i))
        return self._op(eng, fn, wait, sem, inc_n=16, hard=True)

    def wait(self, eng, toks):
        self._op(eng, None, toks, None)

    def emit(self, block):
        for ename in ENGINES:
            steps = self.steps[ename]

            def emitter(e, steps=steps):
                for waits, fn, incs in steps:
                    for (hh, vv) in waits:
                        e.wait_ge(hh, vv)
                    ins = fn(e) if fn is not None else None
                    for (hh, nn) in incs:
                        ins.then_inc(hh, nn)

            getattr(block, ename)(emitter)


class MK:
    def __init__(self, upto="p4", tap=None, nb=BPC, nsc=NSC_B, part="wo",
                 ncc=NCC):
        self.upto = upto
        self.tap = tap
        self.nb = nb
        self.nsc = nsc
        self.part = part
        self.ncc = ncc
        self.nc = bass.Bass("TRN2", target_bir_lowering=False, debug=False)
        self.taps = {}

    def mktap(self, name, shape, dt=F32):
        ap = self.nc.dram_tensor("tap_" + name, shape, dt, kind="ExternalOutput").ap()
        self.taps[name] = ap
        return ap

    def build(self):
        nc = self.nc
        self.x_h = nc.dram_tensor("x", [BPC, T, D], F32, kind="ExternalInput").ap()
        self.w_h = [nc.dram_tensor(n, [D, D], F32, kind="ExternalInput").ap()
                    for n in ("wq", "wk", "wv", "wo")]
        self.w1_h = nc.dram_tensor("w1", [FF, D], F32, kind="ExternalInput").ap()
        self.b1_h = nc.dram_tensor("b1", [FF], F32, kind="ExternalInput").ap()
        self.w2_h = nc.dram_tensor("w2", [D, FF], F32, kind="ExternalInput").ap()
        self.b2_h = nc.dram_tensor("b2", [D], F32, kind="ExternalInput").ap()
        self.c_h = nc.dram_tensor("c", [BPC, D], F32, kind="ExternalInput").ap()
        self.wmod_h = nc.dram_tensor("wmod", [D, 6 * D], F32,
                                     kind="ExternalInput").ap()
        self.out_h = nc.dram_tensor("out", [BPC, T, D], F32,
                                    kind="ExternalOutput").ap()
        self.xb_h = nc.dram_tensor("xbounce", [TPC, D], BF16).ap()
        self.w1b_h = nc.dram_tensor("w1bounce", [FF, D], BF16).ap()
        self.w2b_h = nc.dram_tensor("w2bounce", [D, FF], BF16).ap()
        self.mid_h = nc.dram_tensor("mid", [FF, TPC], F32).ap()
        self.rows_h = nc.dram_tensor("rows", [8, T], BF16).ap()
        self.cci_h = [nc.dram_tensor(f"cc{i}in", [1, 4], F32).ap() for i in (1, 2)]
        self.cco_h = [nc.dram_tensor(f"cc{i}out", [1, 4], F32).ap() for i in (1, 2)]

        with ExitStack() as ctx:
            self.ctx = ctx
            p = Prog(nc, ctx)
            self.p = p
            sb = lambda n, s, dt: ctx.enter_context(nc.sbuf_tensor(n, s, dt))
            ps = lambda n, s, dt: ctx.enter_context(nc.psum_tensor(n, s, dt))
            self.sb = sb

            self.XT = sb("XT", [128, NE * TPC], BF16)
            self.WA = sb("WA", [128, 4 * NE * D], BF16)
            self.ones_col = sb("ones_col", [128, 1], BF16)
            self.ones_row = sb("ones_row", [1, SCT], BF16)
            self.ones_rowf = sb("ones_rowf", [1, 128], F32)
            self.identb = sb("identb", [128, 128], BF16)
            self.identf = sb("identf", [128, 128], F32)
            self.maskb = sb("maskb", [128, 128], BF16)
            self.modcol = sb("modcol", [128, 96], F32)
            self.onepca = sb("onepca", [128, 8 * BPC], F32)
            self.sacol = sb("sacol", [128, 8 * BPC], F32)
            self.onepcb = sb("onepcb", [128, 8 * BPC], F32)
            self.sbcol = sb("sbcol", [128, 8 * BPC], F32)
            self.gacol = sb("gacol", [128, 8 * BPC], F32)
            self.gbcol = sb("gbcol", [128, 8 * BPC], F32)
            self.gtmp = sb("gtmp", [128, 8 * BPC], F32)
            self.gtmp2 = sb("gtmp2", [128, 8 * BPC], F32)
            self.gmx = sb("gmx", [128, 1], F32)
            self.ROWM = sb("ROWM", [1, SCT], F32)
            self.ROWQ = sb("ROWQ", [1, SCT], F32)
            self.ROWT1 = sb("ROWT1", [1, SCT], F32)
            self.ROWT2 = sb("ROWT2", [1, SCT], F32)
            self.ROWB1 = sb("ROWB1", [1, SCT], BF16)
            self.ROWB2 = sb("ROWB2", [1, SCT], BF16)
            self.sw1b = sb("sw1b", [128, 32], F32)
            self.sw2b = sb("sw2b", [128, 8], F32)
            self.b1c = sb("b1c", [128, 32], F32)
            self.b2c = sb("b2c", [128, 8], F32)
            self.bq1c = sb("bq1c", [128, 32], F32)
            self.sc1c = sb("sc1c", [128, 32], F32)
            self.gsc2 = sb("gsc2", [128, 8 * BPC], F32)
            self.gbq2 = sb("gbq2", [128, 8 * BPC], F32)
            self.qt1 = sb("qt1", [128, 32], F32)
            self.qt2 = sb("qt2", [128, 32], F32)
            self.qi32 = sb("qi32", [128, 32], I32)
            self.scl = sb("scl", [128, 4], F32)
            self.mx1 = sb("mx1", [128, 1], F32)
            self.mx2 = sb("mx2", [128, 1], F32)
            self.maxP = sb("maxP", [128, 16], BF16)
            self.minP = sb("minP", [128, 16], BF16)
            self.rc1b = sb("rc1b", [128, 1], BF16)
            self.mpR = sb("mpR", [128, SCT], BF16)
            self.mpB = sb("mpB", [128, SCT], BF16)
            self.mptm = sb("mptm", [128, SCT], BF16)
            self.qacc = sb("qacc", [128, 1], F32)
            self.wqscl = sb("wqscl", [128, 1], F32)
            self.ascol = sb("ascol", [128, 8 * BPC], F32)
            self.bscol = sb("bscol", [128, 8 * BPC], F32)
            self.t8a = sb("t8a", [128, 8], F32)
            self.t8b = sb("t8b", [128, 8], F32)
            self.rc1 = sb("rc1", [128, 1], F32)
            self.ccs = sb("ccs", [1, 4], F32)
            self.silT = sb("silT", [128, NE * BPC], F32)
            self._xsq = [sb("xsq0", [128, SCT], BF16), sb("xsq1", [128, SCT], BF16)]
            self._xsq_free = [None, None]

            self.A01 = ps("A01", [128, 1024], F32)
            self.YB = ps("YB", [128, 1024], F32)
            self.TB = ps("TB", [128, 1024], BF16)
            self.DB = ps("DB", [128, 512], F32)
            self.MS = ps("MS", [128, 512], F32)
            self.P8 = ps("P8", [128, 512], F32)

            self.p0()
            if self.upto != "p0":
                self.p1_all()
            if self.upto in ("p2", "p3", "p4"):
                self.p2()
            if self.upto in ("p3", "p4"):
                self.p3()
            if self.upto == "p4":
                self.p4()
            self.finale()

            with nc.Block() as block:
                p.emit(block)
        return nc

    def p0(self):
        p = self.p
        p0s = ExitStack()
        s0 = lambda n, s_, dt: p0s.enter_context(self.nc.sbuf_tensor(n, s_, dt))
        self.c_sb = s0("c_sb", [BPC, D], F32)
        self.sil = s0("sil", [BPC, D], F32)
        self.modrow = s0("modrow", [BPC, 6 * D], F32)
        b1s = s0("b1s", [32, 128], F32)
        b2s = s0("b2s", [8, 128], F32)
        wmst = [s0("wmst%d" % i, [128, NE * 256], F32) for i in range(2)]

        self.tok_ones = p.g(lambda e: e.memset(self.ones_col[:, :], 1.0),
                            inc=True, hard=True)
        self.tok_onesr = p.g(lambda e: e.memset(self.ones_row[:, :], 1.0),
                             inc=True, hard=True)
        self.tok_onesrf = p.g(lambda e: e.memset(self.ones_rowf[:, :], 1.0),
                              inc=True, hard=True)
        t1 = p.g(lambda e: e.memset(self.identb[:, :], 1.0), inc=True, hard=True)
        self.tok_ib = p.g(lambda e: e.affine_select(
            self.identb[:, :], self.identb[:, :], pattern=[[1, 128]],
            compare_op=ALU.is_equal, fill=0.0, channel_multiplier=-1),
            wait=[t1], inc=True, hard=True)
        t2 = p.g(lambda e: e.memset(self.identf[:, :], 1.0), inc=True, hard=True)
        self.tok_if = p.g(lambda e: e.affine_select(
            self.identf[:, :], self.identf[:, :], pattern=[[1, 128]],
            compare_op=ALU.is_equal, fill=0.0, channel_multiplier=-1),
            wait=[t2], inc=True, hard=True)
        t3 = p.g(lambda e: e.memset(self.maskb[:, :], 1.0), inc=True, hard=True)
        self.tok_mask = p.g(lambda e: e.affine_select(
            self.maskb[:, :], self.maskb[:, :], pattern=[[1, 128]],
            compare_op=ALU.is_ge, fill=0.0, channel_multiplier=-1),
            wait=[t3], inc=True, hard=True)
        self.tok_mx0 = p.g(lambda e: e.memset(self.mx1[:, :], -1e30),
                           inc=True, hard=True)
        self.tok_mx20 = p.g(lambda e: e.memset(self.mx2[:, :], -1e30),
                            inc=True, hard=True)
        self.tok_ccs0 = p.g(lambda e: e.memset(self.ccs[:, :], 0.0),
                            inc=True, hard=True)

        xflat = self.x_h.rearrange("b t d -> (b t) d")
        self.tok_xb = []

        def xb_ring(rc):
            self.tok_xb.append(p.dma(
                "gpsimd", self.xb_h[rc * 1024:(rc + 1) * 1024, :],
                xflat[rc * 1024:(rc + 1) * 1024, :], sem="dxb%d" % rc))

        self.wtoks = []

        def wa_ring(i):
            dst = self.WA[:, i * 8192:(i + 1) * 8192].rearrange(
                "p (je d) -> p je d", je=NE)
            src = self.w_h[i].rearrange("(je p) d -> p je d", p=128)
            self.wtoks.append(p.dma("gpsimd", dst, src, sem="dwa"))

        tok_c = p.dma("sync", self.c_sb[:, :], self.c_h, sem="dcb")
        tok_b1 = p.dma("sync", b1s[:, :], self.b1_h.rearrange("(a j) -> a j", j=128),
                       sem="dcb")
        tok_b2 = p.dma("sync", b2s[:, :], self.b2_h.rearrange("(a j) -> a j", j=128),
                       sem="dcb")
        wmsrc = self.wmod_h.rearrange("(je p) f -> p je f", p=128)

        def wm_dma(r, w):
            buf = wmst[r % 2]
            return p.dma("sync", buf.ap().rearrange("p (je f) -> p je f", je=NE),
                         wmsrc[:, :, r * 256:(r + 1) * 256], wait=w,
                         sem="dwm%d" % (r % 2))
        wm_toks = [wm_dma(0, ()), wm_dma(1, ())]
        xb_ring(0)
        wa_ring(0)

        self.xt_toks = []

        def xt_ring(rc):
            tk = None
            for jf in range(NE):
                tk = p.dma(
                    "sync",
                    self.XT[:, jf * TPC + rc * 1024:jf * TPC + (rc + 1) * 1024],
                    self.xb_h[rc * 1024:(rc + 1) * 1024,
                              jf * 128:(jf + 1) * 128],
                    wait=[self.tok_xb[rc]] if jf == 0 else (),
                    transpose=True, sem="dxt%d" % rc)
            self.xt_toks.append(tk)
        xt_ring(0)

        tb1 = p.t(lambda e: e.transpose(self.MS[:, 0:32], b1s[:, :],
                                        self.identf[0:32, 0:32]),
                  wait=[tok_b1, self.tok_if], inc=True)
        self.tok_b1c = p.v(lambda e: e.tensor_copy(self.b1c[:, :], self.MS[:, 0:32]),
                           wait=[tb1], inc=True, hard=True)
        tb2 = p.t(lambda e: e.transpose(self.MS[:, 32:40], b2s[:, :],
                                        self.identf[0:8, 0:8]),
                  wait=[tok_b2, self.tok_b1c], inc=True)
        self.tok_b2c = p.v(lambda e: e.tensor_copy(self.b2c[:, :], self.MS[:, 32:40]),
                           wait=[tb2], inc=True, hard=True)

        tok_sil = p.a(lambda e: e.activation(self.sil[:, :], self.c_sb[:, :],
                                             AF.Silu), wait=[tok_c], inc=True)
        ttr = None
        for je in range(NE):
            ttr = p.t(lambda e, je=je: e.transpose(
                self.MS[:, 40 + je * BPC:40 + (je + 1) * BPC],
                self.sil[:, je * 128:(je + 1) * 128], self.identf[0:BPC, 0:BPC]),
                wait=[tok_sil, self.tok_b2c], inc=(je == NE - 1))
        tok_silT = p.v(lambda e: e.tensor_copy(self.silT[:, :],
                                               self.MS[:, 40:40 + NE * BPC]),
                       wait=[ttr], inc=True, hard=True)

        self.bank = {"A01": None, "YB": None, "DS": None, "TB": None,
                     "DB": None, "MS": None, "P8": None}
        self.p0_done = []
        self.tok_rows1 = {0: [], 1: []}
        self._rstd_war = []
        self._nmr_war = []
        self.fillers_hi = []
        self.fillers_lo = []
        self._xt_ring = xt_ring

        def sp1(b, pc, deps):
            rt = self.ln_stats_piece(b * T, pc, deps, 2 * b, "drow%d" % b,
                                     self.tok_rows1[b],
                                     dmaeng="sync" if b == 0 else "gpsimd")
            if b == 0:
                self.p0_done += rt

        def mslice(vec, b):
            a = self.modcol.ap().rearrange("p (v j b2) -> p v j b2", v=6, j=8)
            return a[:, vec, :, b]

        self._gate_last = None
        self._ms_ev = None

        def vec_T(v, etok):
            mt = None
            for je in range(NE):
                k = v * 8 + je
                mt = p.t(lambda e, v=v, je=je, k=k: e.transpose(
                    self.MS[:, 100 + k * BPC:100 + (k + 1) * BPC],
                    self.modrow[:, v * D + je * 128: v * D + (je + 1) * 128],
                    self.identf[0:BPC, 0:BPC]),
                    wait=[etok, tok_silT, self.tok_if] if je == 0 else (),
                    inc=(je == NE - 1))
            ev = p.v(lambda e, v=v: e.tensor_copy(
                self.modcol[:, v * 16:(v + 1) * 16],
                self.MS[:, 100 + v * 16:100 + (v + 1) * 16]),
                wait=[mt], inc=True, hard=True)
            self._ms_ev = ev
            return ev

        def gate_chain(v, dcol, ev):
            last = self._gate_last
            for b in range(BPC):
                dst = slice(b * 8, b * 8 + 8)
                c0 = p.v(lambda e, v=v, b=b, dst=dst: e.tensor_copy(
                    self.gtmp2[:, dst], mslice(v, b)),
                    wait=[ev, last], hard=True)
                c1 = p.v(lambda e, dst=dst: e.tensor_scalar(
                    self.gtmp[:, dst], self.gtmp2[:, dst], -1.0, None, ALU.mult),
                    wait=[c0], hard=True)
                c2 = p.v(lambda e, dst=dst: e.tensor_tensor(
                    self.gtmp[:, dst], self.gtmp[:, dst], self.gtmp2[:, dst],
                    ALU.max), wait=[c1], hard=True)
                c3 = p.v(lambda e, dst=dst: e.tensor_scalar(
                    self.gtmp[:, dst], self.gtmp[:, dst], 0.25, None, ALU.max),
                    wait=[c2], hard=True)
                c4 = p.v(lambda e, dst=dst: e.tensor_scalar(
                    self.gtmp2[:, dst], self.gtmp2[:, dst], 0.0, None,
                    ALU.is_ge), wait=[c3], hard=True)
                c5 = p.v(lambda e, dst=dst: e.tensor_scalar(
                    self.gtmp2[:, dst], self.gtmp2[:, dst], 2.0, -1.0,
                    ALU.mult, ALU.add), wait=[c4], hard=True)
                last = p.v(lambda e, dcol=dcol, dst=dst: e.tensor_mul(
                    dcol[:, dst], self.gtmp[:, dst], self.gtmp2[:, dst]),
                    wait=[c5], inc=True, hard=True)
            self._gate_last = last
            return last

        def vec_gates(v, ev):
            if v in (0, 3):
                dcol = self.sacol if v == 0 else self.sbcol
                tk = None
                for b in range(BPC):
                    dst = slice(b * 8, b * 8 + 8)
                    tk = p.v(lambda e, v=v, b=b, dst=dst, dcol=dcol:
                             e.tensor_copy(dcol[:, dst], mslice(v, b)),
                             wait=[ev], inc=(b == BPC - 1), hard=True)
                if v == 0:
                    self._tok_sa = tk
            elif v in (1, 4):
                dcol = self.onepca if v == 1 else self.onepcb
                tk = None
                for b in range(BPC):
                    dst = slice(b * 8, b * 8 + 8)
                    tk = p.v(lambda e, v=v, b=b, dst=dst, dcol=dcol:
                             e.tensor_scalar(dcol[:, dst], mslice(v, b),
                                             1.0, None, ALU.add),
                             wait=[ev], inc=(b == BPC - 1), hard=True)
                if v == 1:
                    self.tok_h1mod = tk
            elif v == 2:
                self.tok_ga = gate_chain(2, self.gacol, ev)
            else:
                self.tok_gb = gate_chain(5, self.gbcol, ev)

        extras = {0: lambda: xb_ring(1), 2: lambda: wa_ring(1),
                  6: lambda: wa_ring(2), 8: lambda: xb_ring(2),
                  10: lambda: wa_ring(3), 12: lambda: xb_ring(3)}
        etok = None
        for r in range(24):
            buf = wmst[r % 2]
            mmtok = None
            for je in range(NE):
                mmtok = p.t(lambda e, je=je, buf=buf: e.matmul(
                    self.DB[0:BPC, 0:256], self.silT[:, je * BPC:(je + 1) * BPC],
                    buf[:, je * 256:(je + 1) * 256],
                    start=(je == 0), stop=(je == NE - 1)),
                    wait=([wm_toks[r], tok_silT, etok] if je == 0 else ()),
                    inc=(je == NE - 1))
            etok = p.v(lambda e, r=r: e.tensor_copy(
                self.modrow[:, r * 256:(r + 1) * 256], self.DB[0:BPC, 0:256]),
                wait=[mmtok], inc=True, hard=True)
            if r + 2 < 24:
                wm_toks.append(wm_dma(r + 2, [mmtok]))
            if r in extras:
                extras[r]()
            if r == 2:
                sp1(0, 0, [])
            elif r == 6:
                sp1(0, 1, [])
            if (r + 1) % 4 == 0:
                vec_done = vec_T((r + 1) // 4 - 1, etok)
                vec_gates((r + 1) // 4 - 1, vec_done)
        tok_mod = etok
        xt_ring(1)
        sp1(0, 2, [])
        sp1(0, 3, [])
        self.tok_gates = self.tok_gb

        self.fillers_hi.append((0.0, lambda: xt_ring(2)))
        self.fillers_hi.append((0.0, lambda: xt_ring(3)))
        for pc in range(NSC_B):
            self.fillers_hi.append((6.0, lambda pc=pc: sp1(1, pc, [])))

        self.bank["DB"] = tok_mod
        self.bank["MS"] = self._ms_ev
        self.p0_done += [tok_mod, self._ms_ev, self.tok_gb, tok_silT,
                         self.tok_b1c, self.tok_b2c]

        if self.tap == "p0":
            p.dma("sync", self.mktap("mod", [BPC, 6 * D]), self.modrow[:, :],
                  wait=[tok_mod])
            p.dma("sync", self.mktap("gates", [128, 8 * BPC]), self.gacol[:, :],
                  wait=[self.tok_gates])
            p.dma("sync", self.mktap("gatesb", [128, 8 * BPC]), self.gbcol[:, :],
                  wait=[self.tok_gates])
            p.dma("sync", self.mktap("xt", [128, 1024], BF16), self.XT[:, 0:1024],
                  wait=[self.xt_toks[0]])
            p.dma("sync", self.mktap("rows", [8, T], BF16), self.rows_h,
                  wait=self.tok_rows1[0] + self.tok_rows1[1])
            p.dma("sync", self.mktap("b1c", [128, 32]), self.b1c[:, :],
                  wait=[self.tok_b1c])
            p.dma("sync", self.mktap("mrow", [1, SCT]), self.ROWM[:, :],
                  wait=list(self.tok_rows1[0]))
            p.dma("sync", self.mktap("qrow", [1, SCT]), self.ROWQ[:, :],
                  wait=list(self.tok_rows1[0]))

        p0s.close()

    def ln_stats(self, xoff, first_deps):
        out_toks = []
        deps = list(first_deps)
        for piece in range(NSC_B):
            out_toks += self.ln_stats_piece(xoff, piece, deps)
            deps = []
        return out_toks

    def ln_stats_piece(self, xoff, piece, deps, rowbase, sem, outlist,
                       dmaeng="sync"):
        p = self.p
        xsq = self._xsq
        bank_free = self.bank.get("P8")
        g0 = xoff + piece * SCT
        xtk = self.xt_toks[g0 // 1024]
        mm_s = None
        for je in range(NE):
            xs = self.XT[:, je * TPC + g0: je * TPC + g0 + SCT]
            mm_s = p.t(lambda e, xs=xs, je=je: e.matmul(
                self.P8[0:1, 0:SCT], self.ones_col[:, :], xs,
                start=(je == 0), stop=(je == NE - 1)),
                wait=[self.tok_ones, xtk] +
                     (deps + [bank_free] if je == 0 else []),
                inc=(je == NE - 1))
        t1 = p.v(lambda e: e.tensor_scalar(self.ROWM[:, :], self.P8[0:1, 0:SCT],
                                           C1_1024, None, ALU.mult),
                 wait=[mm_s], inc=True, hard=True)
        mm_q = None
        for je in range(NE):
            xs = self.XT[:, je * TPC + g0: je * TPC + g0 + SCT]
            sq = xsq[je % 2]
            w_sq = [xtk, self._xsq_free[je % 2]] + (deps if je == 0 else [])
            sqt = p.a(lambda e, xs=xs, sq=sq: e.activation(sq[:, :], xs,
                                                           AF.Square),
                      wait=w_sq, inc=True, hard=True)
            mm_q = p.t(lambda e, sq=sq, je=je: e.matmul(
                self.P8[0:1, 0:SCT], self.ones_col[:, :], sq[:, :],
                start=(je == 0), stop=(je == NE - 1)),
                wait=[sqt] + ([t1] if je == 0 else []),
                inc=True)
            self._xsq_free[je % 2] = mm_q
        t2 = p.v(lambda e: e.tensor_scalar(self.ROWQ[:, :], self.P8[0:1, 0:SCT],
                                           C1_1024, None, ALU.mult),
                 wait=[mm_q], inc=True, hard=True)
        self.bank["P8"] = t2
        t3 = p.v(lambda e: e.tensor_mul(self.ROWT1[:, :], self.ROWM[:, :],
                                        self.ROWM[:, :]),
                 wait=[t1], hard=True)
        t3b = p.v(lambda e: e.tensor_sub(self.ROWT2[:, :], self.ROWQ[:, :],
                                         self.ROWT1[:, :]),
                  wait=[t2, t3], hard=True)
        t4 = p.v(lambda e: e.tensor_scalar(self.ROWT2[:, :], self.ROWT2[:, :],
                                           1e-5, None, ALU.add),
                 wait=[t3b], inc=True, hard=True)
        t5 = p.a(lambda e: e.activation(self.ROWT1[:, :], self.ROWT2[:, :],
                                        AF.Sqrt), wait=[t4], inc=True,
                 hard=True)
        t6 = p.v(lambda e: e.reciprocal(self.ROWT2[:, :], self.ROWT1[:, :]),
                 wait=[t5], inc=True, hard=True)
        t7 = p.v(lambda e: e.tensor_copy(self.ROWB2[:, :], self.ROWT2[:, :]),
                 wait=[t6] + self._rstd_war, inc=True, hard=True)
        d1 = p.dma(dmaeng, self.rows_h[rowbase:rowbase + 1,
                                       piece * SCT:(piece + 1) * SCT],
                   self.ROWB2[:, :], wait=[t7], sem=sem)
        t8 = p.v(lambda e: e.tensor_mul(self.ROWT1[:, :], self.ROWM[:, :],
                                        self.ROWT2[:, :]), wait=[t6],
                 hard=True)
        t9 = p.v(lambda e: e.tensor_scalar(self.ROWT1[:, :], self.ROWT1[:, :],
                                           -1.0, None, ALU.mult),
                 wait=[t8], inc=True, hard=True)
        t9b = p.v(lambda e: e.tensor_copy(self.ROWB1[:, :], self.ROWT1[:, :]),
                  wait=[t9] + self._nmr_war, inc=True, hard=True)
        d2 = p.dma(dmaeng, self.rows_h[rowbase + 1:rowbase + 2,
                                       piece * SCT:(piece + 1) * SCT],
                   self.ROWB1[:, :], wait=[t9b], sem=sem)
        self._rstd_war = [d1]
        self._nmr_war = [d2]
        outlist += [d1, d2]
        return [d1, d2]

    def _pop_fillers(self, budget=45.0):
        while (self.fillers_hi or self.fillers_lo) and budget > 0:
            q = self.fillers_hi if self.fillers_hi else self.fillers_lo
            cost, fn = q.pop(0)
            fn()
            budget -= max(cost, 0.5)

    def _flush_fillers(self):
        while self.fillers_hi or self.fillers_lo:
            q = self.fillers_hi if self.fillers_hi else self.fillers_lo
            q.pop(0)[1]()

    def _sp2_job(self, b, pc):
        deps = list(self.sc_res[(b, pc)])
        self.ln_stats_piece(b * T, pc, deps, 4 + 2 * b, "drw2%d" % b,
                            self.tok_rows2[b], dmaeng="gpsimd")

    def _mp_load(self, b, sc):
        p = self.p
        rbw = self.tok_rows2[b][2 * sc:2 * sc + 2] + self._mp_war
        rb1 = p.dma("gpsimd", self.mpR[:, :],
                    self.rows_h[4 + 2 * b:5 + 2 * b,
                                sc * SCT:(sc + 1) * SCT].broadcast_to(
                                    [128, SCT]), wait=rbw, sem="drb2")
        rb2 = p.dma("gpsimd", self.mpB[:, :],
                    self.rows_h[5 + 2 * b:6 + 2 * b,
                                sc * SCT:(sc + 1) * SCT].broadcast_to(
                                    [128, SCT]), sem="drb2")
        self._mp_rb = [rb1, rb2]

    def _mp_je(self, b, sc, je):
        p = self.p
        g0 = b * T + sc * SCT
        cl = b * 8 + je
        xs = self.XT[:, je * TPC + g0: je * TPC + g0 + SCT]
        res = list(self.sc_res[(b, sc)]) if je == 0 else []
        u1 = p.v(lambda e, xs=xs: e.tensor_mul(self.mptm[:, :], xs,
                                               self.mpR[:, :]),
                 wait=self._mp_rb + res, hard=True)
        u2 = p.v(lambda e: e.tensor_add(self.mptm[:, :], self.mptm[:, :],
                                        self.mpB[:, :]), wait=[u1],
                 hard=True)
        u3 = p.v(lambda e: e.tensor_reduce(self.rc1b[:, :], self.mptm[:, :],
                                           AX, ALU.max), wait=[u2],
                 hard=True)
        if sc == 0:
            u4 = p.v(lambda e, cl=cl: e.tensor_copy(
                self.maxP[:, cl:cl + 1], self.rc1b[:, :]), wait=[u3],
                hard=True)
        else:
            u4 = p.v(lambda e, cl=cl: e.tensor_tensor(
                self.maxP[:, cl:cl + 1], self.maxP[:, cl:cl + 1],
                self.rc1b[:, :], ALU.max), wait=[u3], hard=True)
        u5 = p.v(lambda e: e.tensor_reduce(self.rc1b[:, :], self.mptm[:, :],
                                           AX, ALU.min), wait=[u4],
                 hard=True)
        if sc == 0:
            last = p.v(lambda e, cl=cl: e.tensor_copy(
                self.minP[:, cl:cl + 1], self.rc1b[:, :]), wait=[u5],
                inc=True, hard=True)
        else:
            last = p.v(lambda e, cl=cl: e.tensor_tensor(
                self.minP[:, cl:cl + 1], self.minP[:, cl:cl + 1],
                self.rc1b[:, :], ALU.min), wait=[u5], inc=True, hard=True)
        self._mp_war = [last]

    def _comb_job(self, b):
        p = self.p
        bs = slice(b * 8, b * 8 + 8)
        bp = slice(b * 8, b * 8 + 8)
        m0a = p.v(lambda e, bp=bp: e.tensor_copy(self.t8a[:, :],
                                                  self.maxP[:, bp]),
                   wait=self._mp_war + [self.tok_gb], hard=True)
        m0b = p.v(lambda e, bp=bp: e.tensor_copy(self.t8b[:, :],
                                                  self.minP[:, bp]),
                   wait=[m0a], hard=True)
        m1 = p.v(lambda e, bs=bs: e.tensor_mul(
            self.t8a[:, :], self.t8a[:, :], self.onepcb[:, bs]),
            wait=[m0b], hard=True)
        m2 = p.v(lambda e, bs=bs: e.tensor_mul(
            self.t8b[:, :], self.t8b[:, :], self.onepcb[:, bs]),
            wait=[m1], hard=True)
        m3 = p.v(lambda e: e.tensor_max(self.t8a[:, :], self.t8a[:, :],
                                        self.t8b[:, :]), wait=[m1, m2],
                 hard=True)
        m4 = p.v(lambda e, bs=bs: e.tensor_add(self.t8a[:, :], self.t8a[:, :],
                                               self.sbcol[:, bs]),
                 wait=[m3], hard=True)
        m5 = p.v(lambda e: e.tensor_reduce(self.rc1[:, :], self.t8a[:, :],
                                           AX, ALU.max), wait=[m4], hard=True)
        ms = p.v(lambda e: e.tensor_tensor(self.mx1[:, :], self.mx1[:, :],
                                           self.rc1[:, :], ALU.max),
                 wait=[self.tok_mx0, m5], inc=True, hard=True)
        self.mseq.append(ms)

    def rank1_RB(self, sc, Rdst, Bdst, extra_waits):
        p = self.p
        wR = [self.tok_onesr, self.bank["MS"]] + list(extra_waits)
        mmR = p.t(lambda e, sc=sc: e.matmul(
            self.MS[:, 0:SCT], self.ones_row[0:1, 0:128],
            self.RSTD[0:1, sc * SCT:(sc + 1) * SCT], start=True, stop=True),
            wait=wR, inc=True)
        rb1 = p.v(lambda e, Rdst=Rdst: e.tensor_copy(Rdst[:, :], self.MS[:, 0:SCT]),
                  wait=[mmR], inc=True, hard=True)
        mmB = p.t(lambda e, sc=sc: e.matmul(
            self.MS[:, 0:SCT], self.ones_row[0:1, 0:128],
            self.NMR[0:1, sc * SCT:(sc + 1) * SCT], start=True, stop=True),
            wait=[rb1], inc=True)
        rb2 = p.v(lambda e, Bdst=Bdst: e.tensor_copy(Bdst[:, :], self.MS[:, 0:SCT]),
                  wait=[mmB], inc=True, hard=True)
        self.bank["MS"] = rb2
        return rb1, rb2

    def p1_all(self):
        p = self.p
        p1 = ExitStack()
        s1 = lambda n, s, dt: p1.enter_context(self.nc.sbuf_tensor(n, s, dt))
        self.h1 = s1("h1", [128, NE * SCT], BF16)
        self.qsb = s1("qsb", [128, NE * SCT], BF16)
        self.ksb = s1("ksb", [128, NE * SCT], BF16)
        self.vsb = s1("vsb", [128, NCC * 1024], BF16)
        self.ktok = s1("ktok", [128, 1024], BF16)
        self.msb = s1("msb", [128, 2048], BF16)
        self.ysb = s1("ysb", [128, 1024], BF16)
        self.yT = s1("yT", [128, NE * SCT], BF16)
        self.Ssb = s1("Ssb", [128, 512], F32)
        self.zsb = s1("zsb", [128, 8], F32)
        self.Szb = s1("Szb", [128, 520], BF16)
        self.Rtok = s1("Rtok", [128, 16], F32)
        self.Rtmp = s1("Rtmp", [128, 16], F32)
        self.fm1 = s1("fm1", [128, SCT], BF16)
        self.Rbc = s1("Rbc", [128, SCT], BF16)
        self.Bbc = s1("Bbc", [128, SCT], BF16)

        self.sc_res = {}
        self.mseq = []
        self.tok_rows2 = {0: [], 1: []}
        self._mp_war = []
        self._mp_rb = []
        full = self.part == "wo" and self.upto in ("p2", "p3", "p4")
        self.resid_toks = {0: [], 1: []}
        self._h1_war = []
        self._rb_war = []
        self._h1_pending = None
        self._prev_zmm = None
        self._prev_e3 = None
        self._prev_lastYmm = None
        self._prev_ytr = None
        self._prev_e2 = None
        self._prev_e3 = None

        for b in range(self.nb):
            self.row_toks = list(self.tok_rows1[b]) + \
                (self.p0_done if b == 0 else [])
            zw = [self._prev_zmm] if self._prev_zmm else [self.tok_mx20]
            if b > 0:
                zw += [self.state_tok, self.sadd_tok, self.zadd_tok]
            else:
                zw += self.p0_done
            z1 = p.g(lambda e: e.memset(self.Ssb[:, :], 0.0),
                     wait=zw, inc=True, hard=True)
            z2 = p.g(lambda e: e.memset(self.zsb[:, :], 0.0), inc=True, hard=True)
            z3 = p.g(lambda e: e.memset(self.Szb[:, :], 0.0), inc=True, hard=True)
            self.state_tok = z3
            self.sadd_tok = z1
            self.zadd_tok = z2
            for sc in range(self.nsc):
                self.superchunk(b, sc)
                if full and self.nsc == NSC_B:
                    self.fillers_hi.append(
                        (6.0, lambda b=b, sc=sc: self._sp2_job(b, sc)))
                    self.fillers_hi.append(
                        (1.0, lambda b=b, sc=sc: self._mp_load(b, sc)))
                    for je in range(NE):
                        self.fillers_hi.append(
                            (2.0, lambda b=b, sc=sc, je=je:
                             self._mp_je(b, sc, je)))
                    if sc == NSC_B - 1:
                        self.fillers_hi.append(
                            (2.0, lambda b=b: self._comb_job(b)))
                self._pop_fillers(4.0)
            if self.tap in ("x1", "p1dbg") and b == 0:
                p.dma("sync", self.mktap("x1", [128, 2048], BF16), self.XT[:, 0:2048],
                      wait=self.resid_toks[0][-8:])
        self.p1_done = self.resid_toks[self.nb - 1][-8:]
        self._flush_fillers()
        p1.close()

    def h1_block(self, b, sc):
        p = self.p
        g0 = b * T + sc * SCT
        if b and self.part == "wo" and self.nsc == NSC_B:
            assert len(self.tok_rows1[b]) == 2 * NSC_B
        row_toks = list(self.tok_rows1[b]) + (self.p0_done if b == 0 else [])
        rb1 = p.dma("sync", self.Rbc[:, :],
                    self.rows_h[2 * b:2 * b + 1,
                                sc * SCT:(sc + 1) * SCT].broadcast_to([128, SCT]),
                    wait=row_toks + self._rb_war, sem="drb")
        rb2 = p.dma("sync", self.Bbc[:, :],
                    self.rows_h[2 * b + 1:2 * b + 2,
                                sc * SCT:(sc + 1) * SCT].broadcast_to([128, SCT]),
                    sem="drb")
        h1toks = []
        for je in range(NE):
            xs = self.XT[:, je * TPC + g0: je * TPC + g0 + SCT]
            hs = self.h1[:, je * SCT:(je + 1) * SCT]
            wh = [rb1, rb2, self.xt_toks[g0 // 1024], self.tok_h1mod]
            wh += self._h1_war
            p.v(lambda e, xs=xs, hs=hs: e.tensor_mul(hs, xs, self.Rbc[:, :]),
                wait=wh)
            p.v(lambda e, hs=hs: e.tensor_add(hs, hs, self.Bbc[:, :]))
            h1toks.append(p.v(lambda e, hs=hs, je=je, b=b: e.tensor_scalar(
                hs, hs,
                self.onepca[:, b * 8 + je:b * 8 + je + 1],
                self.sacol[:, b * 8 + je:b * 8 + je + 1], ALU.mult, ALU.add),
                inc=True, hard=True))
        self._rb_war = [h1toks[-1]]
        return h1toks

    def superchunk(self, b, sc):
        p = self.p
        g0 = b * T + sc * SCT
        h1toks = self._h1_pending
        self._h1_pending = None
        if h1toks is None:
            h1toks = self.h1_block(b, sc)

        if self.part == "ln":
            self.resid_toks[b] += h1toks
            return
        projQK = []
        for wi, woff in ((0, 0), (1, 8192)):
            for i in range(NE):
                k = wi * 8 + i
                half = k % 2
                bank = self.A01[:, half * 512:half * 512 + 512]
                mm = None
                for je in range(NE):
                    w = [h1toks[je], self.wtoks[wi]]
                    if je == 0:
                        if len(projQK) >= 2:
                            w.append(projQK[-2])
                        else:
                            w.append(self.bank["A01"])
                    mm = p.t(lambda e, woff=woff, i=i, je=je, bank=bank: e.matmul(
                        bank, self.WA[:, woff + je * 1024 + i * 128:
                                      woff + je * 1024 + (i + 1) * 128],
                        self.h1[:, je * SCT:(je + 1) * SCT],
                        start=(je == 0), stop=(je == NE - 1)),
                        wait=w, inc=(je == NE - 1))
                e2 = p.a(lambda e, bank=bank: e.activation(
                    self.fm1[:, :], bank, AF.Exp),
                    wait=[mm, self._prev_e3], inc=True, hard=True)
                ec = p.v(lambda e: e.tensor_scalar(
                    self.fm1[:, :], self.fm1[:, :], 1.0, None, ALU.min),
                    wait=[e2], hard=True)
                dst = self.qsb if wi == 0 else self.ksb
                e3 = p.v(lambda e, bank=bank, dst=dst, i=i:
                         e.scalar_tensor_tensor(
                             dst[:, i * SCT:(i + 1) * SCT], bank, 1.0,
                             self.fm1[:, :], ALU.add, ALU.max),
                         wait=[ec], inc=True, hard=True)
                self._prev_e3 = e3
                projQK.append(e3)
        self.bank["A01"] = projQK[-1]

        vtoks = []
        for cci in range(NCC):
            for half in range(2):
                bank = self.YB[:, half * 512:half * 512 + 512]
                mm = None
                for je in range(NE):
                    w = [h1toks[je], self.wtoks[2]]
                    if je == 0:
                        if len(vtoks) >= 2:
                            w.append(vtoks[-2])
                        else:
                            w.append(self.bank["YB"])
                    mm = p.t(lambda e, cci=cci, half=half, je=je, bank=bank:
                             e.matmul(
                        bank,
                        self.h1[:, je * SCT + cci * CC: je * SCT + cci * CC + CC],
                        self.WA[:, 16384 + je * 1024 + half * 512:
                                16384 + je * 1024 + (half + 1) * 512],
                        start=(je == 0), stop=(je == NE - 1)),
                        wait=w, inc=(je == NE - 1))
                ev = p.v(lambda e, cci=cci, half=half, bank=bank: e.tensor_copy(
                    self.vsb[:, cci * 1024 + half * 512:
                             cci * 1024 + (half + 1) * 512], bank),
                    wait=[mm], inc=True, hard=True)
                vtoks.append(ev)

        self._h1_war = [projQK[-1], vtoks[-1]]
        if self.tap in ("sc0", "p1dbg") and b == 0 and sc == 0:
            p.dma("sync", self.mktap("h1", [128, NE * SCT], BF16), self.h1[:, :],
                  wait=h1toks)
            p.dma("sync", self.mktap("q", [128, NE * SCT], BF16), self.qsb[:, :],
                  wait=projQK[:8])
            p.dma("sync", self.mktap("k", [128, NE * SCT], BF16), self.ksb[:, :],
                  wait=projQK[8:])
            p.dma("sync", self.mktap("v", [128, NCC * 1024], BF16), self.vsb[:, :],
                  wait=vtoks)

        if self.part == "proj":
            self.resid_toks[b] += [projQK[-1], vtoks[-1]]
            self.bank["YB"] = vtoks[-1]
            return
        ymult_tok = None
        yev_toks = []
        for cci in range(self.ncc):
            ktr = None
            for jd in range(NE):
                w = [projQK[8 + jd], self.tok_ib]
                if jd == 0:
                    w.append(self.bank["TB"])
                ktr = p.t(lambda e, jd=jd, cci=cci: e.transpose(
                    self.TB[:, jd * 128:(jd + 1) * 128],
                    self.ksb[:, jd * SCT + cci * CC: jd * SCT + cci * CC + CC],
                    self.identb[:, :]), wait=w, inc=(jd == NE - 1))
            ktk = p.v(lambda e: e.tensor_copy(self.ktok[:, :], self.TB[:, :]),
                      wait=[ktr, self._prev_zmm], inc=True, hard=True)
            if self.part == "kt":
                self.state_tok = ktk
                self.bank["TB"] = ktk
                continue
            if self.part in ("ata", "atb", "atc", "atd"):
                mm = None
                if self.part == "ata":
                    plan = [(h_, 0, h_ // 2) for h_ in range(8)]
                elif self.part == "atc":
                    plan = [(h_, 64, h_ // 2) for h_ in range(8)]
                elif self.part == "atd":
                    plan = ([(h_, 0, h_) for h_ in range(8)] +
                            [(h_, 64, h_) for h_ in range(8)])
                else:
                    plan = [(h_, (h_ % 2) * 64, h_ // 2) for h_ in range(16)]
                nh = len(plan)
                for idx, (h_, pp, jd) in enumerate(plan):
                    hh = h_ % 8
                    w = list(projQK) + ([self.bank["A01"]] if idx == 0 else [])
                    mm = p.t(lambda e, pp=pp, jd=jd, hh=hh, cci=cci: e.matmul(
                        self.A01[:, hh * 128:(hh + 1) * 128],
                        self.ksb[pp:pp + 64,
                                 jd * SCT + cci * CC: jd * SCT + cci * CC + CC],
                        self.qsb[pp:pp + 64,
                                 jd * SCT + cci * CC: jd * SCT + cci * CC + CC],
                        start=True, stop=True), wait=w,
                        inc=(idx == nh - 1))
                ev = p.v(lambda e: e.tensor_copy(self.msb[:, 0:1024],
                                                 self.A01[:, :]),
                         wait=[mm], inc=True, hard=True)
                self.bank["A01"] = ev
                self.state_tok = ev
                self.bank["TB"] = ktk
                continue
            mev = []
            for r in range(2):
                mm = None
                pp = r * 64
                for jd in range(8):
                    w = []
                    if jd == 0:
                        w = list(projQK)
                        w.append(mev[-1] if mev else self.bank["A01"])
                    mm = p.t(lambda e, pp=pp, jd=jd, cci=cci: e.matmul(
                        self.A01[:, jd * 128:(jd + 1) * 128],
                        self.ksb[pp:pp + 64,
                                 jd * SCT + cci * CC: jd * SCT + cci * CC + CC],
                        self.qsb[pp:pp + 64,
                                 jd * SCT + cci * CC: jd * SCT + cci * CC + CC],
                        start=True, stop=True), wait=w, inc=(jd == 7))
                mc = p.v(lambda e, r=r: e.tensor_copy(
                    self.msb[:, r * 1024:(r + 1) * 1024], self.A01[:, :]),
                    wait=[mm, self._prev_lastYmm], inc=True, hard=True)
                me = p.g(lambda e, r=r: e.affine_select(
                    self.msb[:, r * 1024:(r + 1) * 1024],
                    self.msb[:, r * 1024:(r + 1) * 1024],
                    pattern=[[0, 8], [1, 128]], compare_op=ALU.is_ge,
                    fill=0.0, channel_multiplier=-1),
                    wait=[mc], inc=True, hard=True)
                mev.append(me)
            self.bank["A01"] = mev[-1]
            if self.part == "at":
                self.state_tok = mev[-1]
                self.bank["TB"] = ktk
                continue
            def mcol(h_):
                return (h_ % 2) * 1024 + (h_ // 2) * 128
            last_mm = None
            for h_ in range(16):
                pp = (h_ % 2) * 64
                jd = h_ // 2
                w = [mev[h_ % 2], vtoks[cci * 2 + h_ // 8], self.state_tok]
                if h_ == 0:
                    w.append(ymult_tok if ymult_tok is not None else vtoks[-1])
                    w.append(self.zadd_tok)
                    w.append(self.tok_ones)
                p.t(lambda e, h_=h_, cci=cci: e.matmul(
                    self.YB[:, h_ * 64:(h_ + 1) * 64],
                    self.msb[:, mcol(h_):mcol(h_) + 128],
                    self.vsb[:, cci * 1024 + h_ * 64: cci * 1024 + (h_ + 1) * 64],
                    start=True, stop=False), wait=w)
                p.t(lambda e, h_=h_: e.matmul(
                    self.DB[:, h_:h_ + 1], self.msb[:, mcol(h_):mcol(h_) + 128],
                    self.ones_col[:, :], start=True, stop=False))
                p.t(lambda e, h_=h_, pp=pp, jd=jd, cci=cci: e.matmul(
                    self.YB[:, h_ * 64:(h_ + 1) * 64],
                    self.qsb[pp:pp + 64,
                             jd * SCT + cci * CC: jd * SCT + cci * CC + CC],
                    self.Szb[pp:pp + 64, jd * 64:(jd + 1) * 64],
                    start=False, stop=True))
                last_mm = p.t(lambda e, h_=h_, pp=pp, jd=jd, cci=cci: e.matmul(
                    self.DB[:, h_:h_ + 1],
                    self.qsb[pp:pp + 64,
                             jd * SCT + cci * CC: jd * SCT + cci * CC + CC],
                    self.Szb[pp:pp + 64, 512 + jd:513 + jd],
                    start=False, stop=True), inc=(h_ == 15))
            self._prev_lastYmm = last_mm
            if self.part == "yden":
                ymult_tok = p.v(lambda e: e.tensor_copy(
                    self.ysb[:, :], self.YB[:, :]), wait=[last_mm],
                    inc=True, hard=True)
                self.state_tok = ymult_tok
                self.bank["TB"] = ktk
                continue
            d1 = p.v(lambda e: e.tensor_scalar(self.Rtmp[:, :], self.DB[:, 0:16],
                                               1e-6, None, ALU.add),
                     wait=[last_mm], inc=True, hard=True)
            d2 = p.v(lambda e: e.reciprocal(self.Rtok[:, :], self.Rtmp[:, :]),
                     wait=[d1], inc=True, hard=True)
            rr = self.Rtok.ap().unsqueeze(2).broadcast_to([128, 16, 64])
            ymult_tok = p.v(lambda e, rr=rr: e.tensor_tensor(
                self.ysb.ap().rearrange("p (h e2) -> p h e2", h=16),
                self.YB.ap().rearrange("p (h e2) -> p h e2", h=16),
                rr, ALU.mult), wait=[d2, last_mm, self._prev_ytr],
                inc=True, hard=True)
            ysum = ymult_tok
            if self.part == "ym":
                self.state_tok = ymult_tok
                self.bank["TB"] = ktk
                continue
            ytr = None
            for jd in range(NE):
                w = [ymult_tok, ktk] if jd == 0 else []
                ytr = p.t(lambda e, jd=jd: e.transpose(
                    self.TB[:, jd * 128:(jd + 1) * 128],
                    self.ysb[:, jd * 128:(jd + 1) * 128], self.identb[:, :]),
                    wait=w, inc=(jd == NE - 1))
            self._prev_ytr = ytr
            yev = p.v(lambda e, cci=cci: e.tensor_copy(
                self.yT.ap().rearrange("p (jd t) -> p jd t", jd=NE)[
                    :, :, cci * CC:(cci + 1) * CC],
                self.TB.ap().rearrange("p (jd t) -> p jd t", jd=NE)),
                wait=[ytr], inc=True, hard=True)
            yev_toks.append(yev)
            self.bank["TB"] = yev
            if self.part == "ytr":
                self.state_tok = yev
                continue
            zmm = None
            sadd = None
            for rnd in range(2):
                dsmm = None
                for jj in range(4):
                    j = rnd * 4 + jj
                    w = [ktk, vtoks[cci * 2 + j // 4]]
                    if j == 0:
                        w += [self.sadd_tok, self.bank["MS"], d1]
                    elif jj == 0:
                        w += [sadd]
                    dsmm = p.t(lambda e, j=j, jj=jj, cci=cci: e.matmul(
                        self.MS[:, jj * 128:(jj + 1) * 128],
                        self.ktok[:, j * 128:(j + 1) * 128],
                        self.vsb[:, cci * 1024 + j * 128:
                                 cci * 1024 + (j + 1) * 128],
                        start=True, stop=True), wait=w, inc=(jj == 3))
                    zmm = p.t(lambda e, j=j: e.matmul(
                        self.DB[:, 16 + j:17 + j],
                        self.ktok[:, j * 128:(j + 1) * 128],
                        self.ones_col[:, :], start=True, stop=True),
                        inc=(j == 7))
                off = rnd * 4 * 64
                sa_dst0 = self.Ssb.ap()[0:64, off:off + 256].rearrange(
                    "p (j q) -> p j q", j=4)
                sa_src0 = self.MS.ap()[0:64, :].rearrange(
                    "p (j q) -> p j q", j=4)[:, :, 0:64]
                sa_dst1 = self.Ssb.ap()[64:128, off:off + 256].rearrange(
                    "p (j q) -> p j q", j=4)
                sa_src1 = self.MS.ap()[64:128, :].rearrange(
                    "p (j q) -> p j q", j=4)[:, :, 64:128]
                p.v(lambda e, sa_dst0=sa_dst0, sa_src0=sa_src0: e.tensor_tensor(
                    sa_dst0, sa_dst0, sa_src0, ALU.add),
                    wait=[dsmm], hard=True)
                sadd = p.v(lambda e, sa_dst1=sa_dst1, sa_src1=sa_src1:
                           e.tensor_tensor(sa_dst1, sa_dst1, sa_src1, ALU.add),
                           inc=True, hard=True)
            self._prev_zmm = zmm
            self.bank["MS"] = sadd
            zadd = p.v(lambda e: e.tensor_add(self.zsb[:, :], self.zsb[:, :],
                                              self.DB[:, 16:24]),
                       wait=[self.state_tok, zmm], inc=True, hard=True)
            p.v(lambda e: e.tensor_copy(self.Szb[:, 0:512], self.Ssb[:, :]),
                wait=[sadd])
            sc2 = p.v(lambda e: e.tensor_copy(self.Szb[:, 512:520],
                                              self.zsb[:, :]),
                      wait=[zadd], inc=True, hard=True)
            self.sadd_tok = sadd
            self.zadd_tok = zadd
            self.state_tok = sc2
            self.bank["A01"] = mev[-1]

            if self.part == "wo" and self.nsc == NSC_B:
                if cci == 0:
                    nxt = b * self.nsc + sc + 1
                    if nxt < self.nb * self.nsc:
                        self._h1_pending = self.h1_block(nxt // NSC_B,
                                                         nxt % NSC_B)
                else:
                    self._pop_fillers(7.0)

            if self.tap in ("y0", "p1dbg") and b == 0 and sc == 0 and cci == 0:
                p.dma("sync", self.mktap("y0", [128, 1024], BF16), self.ysb[:, :],
                      wait=[ymult_tok])
                p.dma("sync", self.mktap("den0", [128, 16]), self.Rtmp[:, :],
                      wait=[d1])
                p.dma("sync", self.mktap("ktok0", [128, 1024], BF16), self.ktok[:, :],
                      wait=[ktk])
                p.dma("sync", self.mktap("S0", [128, 512]), self.Ssb[:, :],
                      wait=[sadd])

        if self.part in ("scan", "kt", "at", "ata", "atb", "atc", "atd", "yden", "ym", "ytr"):
            self.resid_toks[b] += [self.state_tok]
            self.bank["YB"] = ymult_tok if ymult_tok else vtoks[-1]
            return
        res8 = []
        for i in range(NE):
            half = i % 2
            bank = self.YB[:, half * 512:half * 512 + 512]
            mm = None
            for je in range(NE):
                w = [yev_toks[-1], self.wtoks[3]]
                if je == 0:
                    w.append(res8[-2] if len(res8) >= 2 else ymult_tok)
                mm = p.t(lambda e, i=i, je=je, bank=bank: e.matmul(
                    bank, self.WA[:, 24576 + je * 1024 + i * 128:
                                  24576 + je * 1024 + (i + 1) * 128],
                    self.yT[:, je * SCT:(je + 1) * SCT],
                    start=(je == 0), stop=(je == NE - 1)),
                    wait=w, inc=(je == NE - 1))
            res = p.v(lambda e, i=i, bank=bank, b=b, g0=g0: e.scalar_tensor_tensor(
                self.XT[:, i * TPC + g0: i * TPC + g0 + SCT], bank,
                self.gacol[:, b * 8 + i:b * 8 + i + 1],
                self.XT[:, i * TPC + g0: i * TPC + g0 + SCT],
                ALU.mult, ALU.add), wait=[mm, self.tok_ga],
                inc=True, hard=True)
            res8.append(res)
        self.resid_toks[b] += res8
        self.sc_res[(b, sc)] = res8
        self.bank["YB"] = res8[-1]

    def p2(self):
        p = self.p
        pd = self.p1_done
        p2 = ExitStack()
        s2 = lambda n, s_, dt: p2.enter_context(self.nc.sbuf_tensor(n, s_, dt))
        wqI1 = s2("wqI1", [128, D], I8)
        wqF1 = [s2("wqF0", [128, D], F32), s2("wqF1", [128, D], F32)]
        wqB1 = [s2("wqB0", [128, D], BF16), s2("wqB1", [128, D], BF16)]
        self._wqF2 = s2("wqF2", [128, FF], F32)
        self._wqI2 = s2("wqI2", [128, FF], I8)

        self._wq1 = dict(wqF=wqF1, wqI=[wqI1], wqB=wqB1,
                         buf_free=[list(pd), list(pd)],
                         st_free=[None, None], q3_free=[None], ai=0,
                         cast_store=False)
        for a in range(NF):
            st = self.wq_tile(self._wq1, self.w1_h, self.w1b_h, a, 1024,
                              self.sw1b)
        self.tok_w1q = st
        self._p2_scope = p2
        f1 = p.v(lambda e: e.tensor_scalar(self.sw1b[:, :], self.sw1b[:, :],
                                           C127I, None, ALU.mult),
                 wait=[self.tok_w1q], inc=True, hard=True)
        self.tok_sw = [f1]

        self.w1t_toks = []
        for jd in range(NE):
            self.w1t_toks.append(p.dma(
                "sync", self.WA[:, jd * 4096:(jd + 1) * 4096],
                self.w1b_h[:, jd * 128:(jd + 1) * 128],
                wait=[self.tok_w1q] + pd, transpose=True, sem="dw1t"))

        tokc = self.cross_max(self.mx1, list(self.mseq), 0)
        self.tok_s1 = self.scale_consts(tokc, 0)
        q1 = p.v(lambda e: e.tensor_scalar(self.sc1c[:, :], self.sw1b[:, :],
                                           self.scl[:, 1:2], None, ALU.mult),
                 wait=[self.tok_s1, f1], inc=True, hard=True)
        q2 = p.v(lambda e: e.reciprocal(self.qt1[:, :], self.sc1c[:, :]),
                 wait=[q1], inc=True, hard=True)
        q3 = p.v(lambda e: e.tensor_mul(self.qt2[:, :], self.b1c[:, :],
                                        self.qt1[:, :]),
                 wait=[q2, self.tok_b1c], inc=True)
        q4 = p.v(lambda e: e.tensor_copy(self.qi32[:, :], self.qt2[:, :]),
                 wait=[q3], inc=True)
        q5 = p.v(lambda e: e.tensor_copy(self.qt2[:, :], self.qi32[:, :]),
                 wait=[q4], inc=True)
        self.tok_bq1 = p.v(lambda e: e.tensor_mul(self.bq1c[:, :], self.qt2[:, :],
                                                  self.sc1c[:, :]),
                           wait=[q5], inc=True, hard=True)
        a1 = p.v(lambda e: e.tensor_scalar(self.ascol[:, :], self.onepcb[:, :],
                                           self.scl[:, 0:1], None, ALU.mult),
                 wait=[self.tok_s1, self.tok_gb], inc=True, hard=True)
        self.tok_asb = p.v(lambda e: e.tensor_scalar(
            self.bscol[:, :], self.sbcol[:, :], self.scl[:, 0:1], None,
            ALU.mult), wait=[a1], inc=True, hard=True)
        self.p2_done = [self.tok_bq1, self.mseq[-1]]

        if self.tap == "ffn":
            p.dma("sync", self.mktap("scl", [128, 4]), self.scl[:, :],
                  wait=[self.tok_s1])
            p.dma("sync", self.mktap("sw1", [128, 32]), self.sw1b[:, :], wait=[f1])
            p.dma("sync", self.mktap("bq1", [128, 32]), self.bq1c[:, :],
                  wait=[self.tok_bq1])
            p.dma("sync", self.mktap("mx1", [128, 1]), self.mx1[:, :],
                  wait=[self.mseq[-1]])

    def wq_tile(self, wq, w_src, w_dst, a, ncol, swc):
        p = self.p
        par = wq["ai"] % len(wq["wqF"])
        ipar = wq["ai"] % len(wq["wqI"])
        wF = wq["wqF"][par]
        wI = wq["wqI"][ipar]
        ld = p.dma("sync", wF[:, 0:ncol], w_src[a * 128:(a + 1) * 128, :],
                   wait=wq["buf_free"][par], sem="dwq%d" % par)
        am = p.v(lambda e, ncol=ncol, wF=wF: e.tensor_reduce(
            self.rc1[:, :], wF[:, 0:ncol], AX, ALU.max,
            apply_absolute_value=True), wait=[ld], inc=True, hard=True)
        am2 = p.v(lambda e, swc=swc, a=a: e.tensor_scalar(
            swc[:, a:a + 1], self.rc1[:, :], 1e-8, None, ALU.max),
            wait=[am], inc=True, hard=True)
        rq0 = p.v(lambda e, swc=swc, a=a: e.reciprocal(
            self.rc1[:, :], swc[:, a:a + 1]), wait=[am2], hard=True)
        rq = p.v(lambda e: e.tensor_scalar(self.rc1[:, :], self.rc1[:, :],
                                           127.0, None, ALU.mult),
                 wait=[rq0], inc=True, hard=True)
        if wq["cast_store"]:
            q = p.v(lambda e, ncol=ncol, wF=wF, wI=wI: e.tensor_scalar(
                wI[:, 0:ncol], wF[:, 0:ncol], self.rc1[:, 0:1], None,
                ALU.mult), wait=[rq, wq["st_free"][0]], inc=True, hard=True)
            st = p.dma("gpsimd", w_dst[a * 128:(a + 1) * 128, :],
                       wI[:, 0:ncol], wait=[q], sem="dwsc")
            wq["st_free"][0] = st
        else:
            q = p.v(lambda e, ncol=ncol, wF=wF, wI=wI: e.tensor_scalar(
                wI[:, 0:ncol], wF[:, 0:ncol], self.rc1[:, 0:1], None,
                ALU.mult), wait=[rq, wq["q3_free"][0]], inc=True, hard=True)
            wB = wq["wqB"][par]
            q3 = p.a(lambda e, ncol=ncol, wB=wB, wI=wI: e.activation(
                wB[:, 0:ncol], wI[:, 0:ncol], AF.Copy),
                wait=[q, wq["st_free"][par]], inc=True, hard=True)
            wq["q3_free"][0] = q3
            st = p.dma("scalar", w_dst[a * 128:(a + 1) * 128, :],
                       wB[:, 0:ncol], wait=[q3], sem="dws%d" % par)
            wq["st_free"][par] = st
        wq["buf_free"][par] = [q]
        wq["ai"] += 1
        return st

    def cross_max(self, mxcol, deps, idx):
        p = self.p
        tt = p.t(lambda e, mxcol=mxcol: e.transpose(
            self.MS[0:1, 0:128], mxcol[:, :], self.identf[:, :]),
            wait=list(deps) + [self.tok_if, self.bank["MS"]], inc=True)
        e1 = p.v(lambda e: e.tensor_reduce(self.ccs[0:1, 0:1],
                                           self.MS[0:1, 0:128], AX, ALU.max),
                 wait=[tt, self.tok_ccs0], inc=True, hard=True)
        self.bank["MS"] = e1
        d1 = p.dma("gpsimd", self.cci_h[idx], self.ccs[0:1, :],
                   wait=[e1, self.tok_ccs0], sem="dcc")
        cc = p.g(lambda e, idx=idx: e.collective_compute(
            "AllReduce", ALU.max, replica_groups=[list(range(8))],
            ins=[self.cci_h[idx].opt()], outs=[self.cco_h[idx].opt()]),
            wait=[d1], inc=True, hard=True)
        d2 = p.dma("gpsimd", self.ccs[0:1, :], self.cco_h[idx], wait=[cc],
                   sem="dcc")
        return d2

    def scale_consts(self, tokc, pairidx):
        p = self.p
        c0, c1 = 2 * pairidx, 2 * pairidx + 1
        e1 = p.v(lambda e: e.tensor_single_scalar(
            self.ccs[0:1, 1:2], self.ccs[0:1, 0:1], 1e-8, ALU.max),
            wait=[tokc], inc=True, hard=True)
        e2 = p.v(lambda e: e.tensor_single_scalar(
            self.ccs[0:1, 2:3], self.ccs[0:1, 1:2], C255I, ALU.mult),
            wait=[e1], inc=True, hard=True)
        e3 = p.v(lambda e: e.reciprocal(self.ccs[0:1, 3:4], self.ccs[0:1, 2:3]),
                 wait=[e2], inc=True, hard=True)
        mm1 = p.t(lambda e: e.matmul(self.MS[:, 0:1], self.ones_rowf[0:1, :],
                                     self.ccs[0:1, 2:3], start=True, stop=True),
                  wait=[e2, e3, self.tok_onesrf, self.bank["MS"]], inc=True)
        mm2 = p.t(lambda e: e.matmul(self.MS[:, 1:2], self.ones_rowf[0:1, :],
                                     self.ccs[0:1, 3:4], start=True, stop=True),
                  inc=True)
        ev = p.v(lambda e, c0=c0, c1=c1: e.tensor_copy(
            self.scl[:, c1:c1 + 1], self.MS[:, 0:1]),
            wait=[mm1, mm2], inc=True, hard=True)
        ev2 = p.v(lambda e, c0=c0: e.tensor_copy(self.scl[:, c0:c0 + 1],
                                                 self.MS[:, 1:2]),
                  wait=[ev], inc=True, hard=True)
        self.bank["MS"] = ev2
        return ev2

    def p3(self):
        p = self.p
        nc = self.nc
        p3 = ExitStack()
        s3 = lambda n, s, dt: p3.enter_context(nc.sbuf_tensor(n, s, dt))
        Rb = s3("Rb3", [128, SCT], BF16)
        Bb = s3("Bb3", [128, SCT], BF16)
        tmf = s3("tmf3", [128, SCT], F32)
        hu8 = s3("hu8", [128, SCT], U8)
        h2qb = [s3("h2q0", [128, NE * SCT], BF16),
                s3("h2q1", [128, NE * SCT], BF16)]
        mst = [s3("mst0", [128, SCT], F32), s3("mst1", [128, SCT], F32)]

        pd = self.p2_done
        mst_free = [None, None]
        self._last_gmax = self.tok_mx20
        self._last_middma = None
        rows2 = self.tok_rows2[0] + self.tok_rows2[1]
        NCH = BPC * NSC_B
        h2q_toks = [[None] * NE, [None] * NE]
        self._p3_mm_par = [None, None]
        self._h2q_last = [None, None]
        self._rbprev = None

        def emit_h2q(ci):
            b, sc = divmod(ci, NSC_B)
            g0 = b * T + sc * SCT
            h2q = h2qb[ci % 2]
            prev_par_mm = self._p3_mm_par[ci % 2]
            w_rb = rows2 + pd
            if self._rbprev is not None:
                w_rb.append(self._rbprev)
            rb1 = p.dma("sync", Rb[:, :],
                        self.rows_h[4 + 2 * b:5 + 2 * b,
                                    sc * SCT:(sc + 1) * SCT].broadcast_to(
                                        [128, SCT]), wait=w_rb, sem="drb")
            rb2 = p.dma("sync", Bb[:, :],
                        self.rows_h[5 + 2 * b:6 + 2 * b,
                                    sc * SCT:(sc + 1) * SCT].broadcast_to(
                                        [128, SCT]), sem="drb")
            for je in range(NE):
                xs = self.XT[:, je * TPC + g0: je * TPC + g0 + SCT]
                w = [rb1, rb2] + pd
                if prev_par_mm is not None:
                    w.append(prev_par_mm)
                v1 = p.v(lambda e, xs=xs: e.tensor_mul(tmf[:, :], xs,
                                                       Rb[:, :]),
                         wait=w, hard=True)
                v2 = p.v(lambda e: e.tensor_add(tmf[:, :], tmf[:, :],
                                                Bb[:, :]), wait=[v1],
                         hard=True)
                cl = b * 8 + je
                vq = p.v(lambda e, cl=cl: e.tensor_scalar(
                    hu8[:, :], tmf[:, :],
                    self.ascol[:, cl:cl + 1], self.bscol[:, cl:cl + 1],
                    ALU.mult, ALU.add), wait=[self.tok_asb, v2], hard=True)
                h2q_toks[ci % 2][je] = p.v(
                    lambda e, je=je, h2q=h2q: e.tensor_copy(
                        h2q[:, je * SCT:(je + 1) * SCT], hu8[:, :]),
                    wait=[vq], inc=True, hard=True)
            self._h2q_last[ci % 2] = h2q_toks[ci % 2][NE - 1]
            self._rbprev = h2q_toks[ci % 2][NE - 1]

        emit_h2q(0)
        p3banks = [self.A01[:, 0:512], self.A01[:, 512:1024],
                   self.YB[:, 0:512], self.YB[:, 512:1024]]
        self._wq2 = dict(wqF=[self._wqF2], wqI=[self._wqI2],
                         buf_free=[self._wq1["buf_free"][0] +
                                   self._wq1["buf_free"][1]],
                         st_free=[self._wq1["q3_free"][0]], ai=0,
                         cast_store=True)
        mm = None
        prev_ep = [None, None, None, None]
        for ci in range(NCH):
            if ci + 1 < NCH:
                emit_h2q(ci + 1)
            if ci < NE:
                self.tok_w2q = self.wq_tile(self._wq2, self.w2_h,
                                            self.w2b_h, ci, 4096, self.sw2b)
            b, sc = divmod(ci, NSC_B)
            g0 = b * T + sc * SCT
            h2q = h2qb[ci % 2]
            for fi in range(NF):
                half = fi % 4
                bank = p3banks[half]
                mm = None
                for jd in range(NE):
                    w = [h2q_toks[ci % 2][jd], self.w1t_toks[jd]]
                    if jd == 0:
                        w.append(prev_ep[half] if prev_ep[half]
                                 else (self.bank["A01"] if half < 2
                                       else self.bank["YB"]))
                    mm = p.t(lambda e, fi=fi, jd=jd, bank=bank, h2q=h2q:
                             e.matmul(
                        bank, self.WA[:, jd * 4096 + fi * 128:
                                      jd * 4096 + (fi + 1) * 128],
                        h2q[:, jd * SCT:(jd + 1) * SCT],
                        start=(jd == 0), stop=(jd == NE - 1)),
                        wait=w, inc=(jd == NE - 1))
                stg = mst[fi % 2]
                w_ep = [mm, self.tok_bq1] + (mst_free[fi % 2] or [])
                ep = p.a(lambda e, fi=fi, bank=bank, stg=stg: e.activation(
                    stg[:, :], bank, AF.Relu,
                    bias=self.bq1c[:, fi:fi + 1],
                    scale=self.sc1c[:, fi:fi + 1]),
                    wait=w_ep, inc=True, hard=True)
                prev_ep[half] = ep
                dm = p.dma("scalar",
                           self.mid_h[fi * 128:(fi + 1) * 128, g0:g0 + SCT],
                           stg[:, :], wait=[ep], sem="dms%d" % (fi % 2))
                gm = p.v(lambda e, stg=stg: e.tensor_reduce(
                    self.gmx[:, :], stg[:, :], AX, ALU.max),
                    wait=[ep], inc=True, hard=True)
                gm2 = p.v(lambda e: e.tensor_tensor(
                    self.mx2[:, :], self.mx2[:, :], self.gmx[:, :],
                    ALU.max), wait=[self.tok_mx20, gm], inc=True, hard=True)
                self._last_gmax = gm2
                self._last_middma = dm
                mst_free[fi % 2] = [dm, gm2]
            self.bank["A01"] = prev_ep[3]
            self.bank["YB"] = prev_ep[3]
            self._p3_mm_par[ci % 2] = mm
        self._p3_last_mm = mm
        self._p3_last_mm = mm
        f2 = p.v(lambda e: e.tensor_scalar(self.sw2b[:, :], self.sw2b[:, :],
                                           C127I, None, ALU.mult),
                 wait=[self.tok_w2q], inc=True, hard=True)
        self.tok_sw = self.tok_sw + [f2]

        tokc = self.cross_max(self.mx2, [self._last_gmax, self._last_middma], 1)
        self.tok_s2 = self.scale_consts(tokc, 1)
        q1 = p.v(lambda e: e.tensor_scalar(self.t8a[:, :], self.sw2b[:, :],
                                           self.scl[:, 3:4], None, ALU.mult),
                 wait=[self.tok_s2] + self.tok_sw, inc=True, hard=True)
        q2 = p.v(lambda e: e.reciprocal(self.t8b[:, :], self.t8a[:, :]),
                 wait=[q1], inc=True, hard=True)
        q3 = p.v(lambda e: e.tensor_mul(self.qt1[:, 0:8], self.b2c[:, :],
                                        self.t8b[:, :]),
                 wait=[q2, self.tok_b2c], inc=True)
        q4 = p.v(lambda e: e.tensor_copy(self.qi32[:, 0:8], self.qt1[:, 0:8]),
                 wait=[q3], inc=True)
        q5 = p.v(lambda e: e.tensor_copy(self.qt1[:, 0:8], self.qi32[:, 0:8]),
                 wait=[q4], inc=True)
        bq2 = p.v(lambda e: e.tensor_mul(self.qt2[:, 0:8], self.qt1[:, 0:8],
                                         self.t8a[:, :]),
                  wait=[q5], inc=True, hard=True)
        gtoks = []
        for b in range(BPC):
            bs = slice(b * 8, b * 8 + 8)
            gtoks.append(p.v(lambda e, bs=bs: e.tensor_mul(
                self.gsc2[:, bs], self.t8a[:, :], self.gbcol[:, bs]),
                wait=[q1, self.tok_gates], inc=True, hard=True))
            gtoks.append(p.v(lambda e, bs=bs: e.tensor_mul(
                self.gbq2[:, bs], self.qt2[:, 0:8], self.gbcol[:, bs]),
                wait=[bq2], inc=True, hard=True))
        self.tok_g2 = gtoks
        self.p3_done = [self._p3_last_mm, self._last_middma, self._last_gmax,
                        gtoks[-1], self.tok_w2q]
        p3.close()
        self._p2_scope.close()

    def p4(self):
        p = self.p
        nc = self.nc
        p4 = ExitStack()
        s4 = lambda n, s, dt: p4.enter_context(nc.sbuf_tensor(n, s, dt))
        WB = self.WA
        mq = s4("mq", [128, NF * 512], BF16)
        mld = [s4("mld0", [128, SCT], F32), s4("mld1", [128, SCT], F32)]
        mtf = s4("mtf", [128, SCT], F32)
        mu8 = s4("mu8", [128, SCT], U8)
        xps = s4("xps", [128, SCT], BF16)
        ost = s4("ost", [128, NE * SCT], BF16)
        ost2 = s4("ost2", [128, 1024], F32)

        pd = self.p3_done
        w2t = []
        for jf in range(NF):
            w2t.append(p.dma(
                "sync", WB[:, jf * 1024:(jf + 1) * 1024],
                self.w2b_h[:, jf * 128:(jf + 1) * 128],
                wait=[self.tok_w2q] + pd, transpose=True, sem="dwb"))

        ev_toks = [None, None, None, None]
        tr_ev = None
        prev_out_dma = None
        last_out = None
        banks4 = [self.A01[:, 0:512], self.A01[:, 512:1024],
                  self.YB[:, 0:512], self.YB[:, 512:1024]]
        g2_jf = [None] * NF
        self._p4q = {}
        self._p4_lm = 0

        def quant_tile(jf, g0):
            buf = mld[self._p4_lm % 2]
            self._p4_lm += 1
            w = [self._last_middma] + pd
            if self._p4_mq2ago[0] is not None:
                w.append(self._p4_mq2ago[0])
            lm = p.dma("sync", buf[:, :],
                       self.mid_h[jf * 128:(jf + 1) * 128, g0:g0 + SCT],
                       wait=w, sem="dml%d" % (self._p4_lm % 2))
            n1 = p.v(lambda e, buf=buf: e.tensor_scalar(
                mtf[:, :], buf[:, :], self.scl[:, 2:3], None, ALU.mult),
                wait=[lm, self.tok_s2], hard=True)
            n2 = p.v(lambda e: e.tensor_copy(mu8[:, :], mtf[:, :]),
                     wait=[n1], hard=True)
            wq = [n2]
            if g2_jf[jf] is not None:
                wq.append(g2_jf[jf])
            tok = p.v(lambda e, jf=jf: e.tensor_copy(
                mq[:, jf * 512:(jf + 1) * 512], mu8[:, :]),
                wait=wq, inc=True, hard=True)
            self._p4_mq2ago = [self._p4_mq2ago[1], tok]
            return tok

        self._p4_mq2ago = [None, None]
        for b in range(BPC):
            for sc in range(NSC_B):
                g0 = b * T + sc * SCT
                mq_toks = {}
                g1_mm = None
                for jf in range(NF):
                    if jf in self._p4q:
                        mq_toks[jf] = self._p4q.pop(jf)
                    else:
                        mq_toks[jf] = quant_tile(jf, g0)
                    for i in range(4):
                        w2 = [mq_toks[jf], w2t[jf]]
                        if jf == 0:
                            w2.append(ev_toks[i] if ev_toks[i]
                                      else self.bank["YB"])
                            w2.append(self.bank["A01"])
                        g1_mm = p.t(lambda e, i=i, jf=jf: e.matmul(
                            banks4[i], WB[:, jf * 1024 + i * 128:
                                          jf * 1024 + (i + 1) * 128],
                            mq[:, jf * 512:(jf + 1) * 512],
                            start=(jf == 0), stop=(jf == NF - 1)),
                            wait=w2, inc=(jf == NF - 1 and i == 3))
                for i in range(4):
                    xp = p.v(lambda e, i=i, b=b, g0=g0: e.tensor_scalar(
                        xps[:, :], self.XT[:, i * TPC + g0: i * TPC + g0 + SCT],
                        self.gbq2[:, b * 8 + i:b * 8 + i + 1], None, ALU.add),
                        wait=self.tok_g2 + [g1_mm], hard=True)
                    ev_toks[i] = p.v(lambda e, i=i, b=b: e.scalar_tensor_tensor(
                        ost[:, i * SCT:(i + 1) * SCT], banks4[i],
                        self.gsc2[:, b * 8 + i:b * 8 + i + 1],
                        xps[:, :], ALU.mult, ALU.add),
                        wait=[xp], inc=True, hard=True)
                g2_mm = None
                for jf in range(NF):
                    for i in range(4):
                        w2 = [mq_toks[jf]]
                        if jf == 0:
                            w2.append(ev_toks[i])
                        g2_mm = p.t(lambda e, i=i, jf=jf: e.matmul(
                            banks4[i], WB[:, jf * 1024 + (i + 4) * 128:
                                          jf * 1024 + (i + 5) * 128],
                            mq[:, jf * 512:(jf + 1) * 512],
                            start=(jf == 0), stop=(jf == NF - 1)),
                            wait=w2, inc=(i == 3))
                    g2_jf[jf] = g2_mm
                for i in range(4):
                    xp = p.v(lambda e, i=i, b=b, g0=g0: e.tensor_scalar(
                        xps[:, :],
                        self.XT[:, (i + 4) * TPC + g0: (i + 4) * TPC + g0 + SCT],
                        self.gbq2[:, b * 8 + i + 4:b * 8 + i + 5], None,
                        ALU.add), wait=self.tok_g2 + [g2_mm], hard=True)
                    ev_toks[i] = p.v(lambda e, i=i, b=b:
                                     e.scalar_tensor_tensor(
                        ost[:, (i + 4) * SCT:(i + 5) * SCT], banks4[i],
                        self.gsc2[:, b * 8 + i + 4:b * 8 + i + 5],
                        xps[:, :], ALU.mult, ALU.add),
                        wait=[xp], inc=True, hard=True)
                last_ev = ev_toks[3]
                nxt = b * NSC_B + sc + 1
                if nxt < BPC * NSC_B:
                    ng0 = (nxt // NSC_B) * T + (nxt % NSC_B) * SCT
                    for jf in range(6):
                        self._p4q[jf] = quant_tile(jf, ng0)
                for cci in range(NCC):
                    tr = None
                    for i in range(NE):
                        w = [last_ev, self.tok_ib]
                        if i == 0 and tr_ev is not None:
                            w.append(tr_ev)
                        tr = p.t(lambda e, i=i, cci=cci: e.transpose(
                            self.TB[:, i * 128:(i + 1) * 128],
                            ost[:, i * SCT + cci * CC: i * SCT + cci * CC + CC],
                            self.identb[:, :]), wait=w, inc=(i == NE - 1))
                    tr_ev = p.v(lambda e: e.tensor_copy(ost2[:, :],
                                                        self.TB[:, :]),
                                wait=[tr, prev_out_dma], inc=True, hard=True)
                    t0 = sc * SCT + cci * CC
                    last_out = p.dma("sync", self.out_h[b, t0:t0 + CC, :],
                                     ost2[:, :], wait=[tr_ev], sem="dout")
                    prev_out_dma = last_out
                self.bank["TB"] = tr_ev
                self.bank["A01"] = last_ev
                self.bank["YB"] = last_ev
        self.final_out = last_out
        p4.close()

    def finale(self):
        p = self.p
        toks = [(sem, p.semc[sem], True) for sem in sorted(p.dma_sems)
                if p.semc[sem]]
        p.wait("sync", toks)


def build(upto="p4", tap=None, nb=BPC, nsc=NSC_B, part="wo", ncc=NCC):
    mk = MK(upto=upto, tap=tap, nb=nb, nsc=nsc, part=part, ncc=ncc)
    nc = mk.build()
    return nc, mk.taps


def kernel(**inputs):
    nc, _ = build()
    in_maps = []
    x = np.ascontiguousarray(inputs["L0_attn_x"])
    for k in range(8):
        in_maps.append({
            "x": x[2 * k:2 * k + 2],
            "wq": inputs["L0_attn_Wq"], "wk": inputs["L0_attn_Wk"],
            "wv": inputs["L0_attn_Wv"], "wo": inputs["L0_attn_Wo"],
            "w1": inputs["L0_ffn_w1"], "b1": inputs["L0_ffn_b1"],
            "w2": inputs["L0_ffn_w2"], "b2": inputs["L0_ffn_b2"],
            "c": inputs["L0_c"][2 * k:2 * k + 2],
            "wmod": inputs["L0_wmod"],
        })
    res = run_bass_kernel_spmd(nc, in_maps, core_ids=list(range(8)))
    out = np.empty((B, T, D), np.float32)
    for k in range(8):
        out[2 * k:2 * k + 2] = res.results[k]["out"]
    return out
